# version 05
# baseline (speedup 1.0000x reference)
import contextlib
import numpy as np
import ml_dtypes
import concourse.bass as bass
import concourse.mybir as mybir
from concourse.bass_utils import run_bass_kernel_spmd

F32 = mybir.dt.float32
BF16 = mybir.dt.bfloat16
U32 = mybir.dt.uint32
AF = mybir.ActivationFunctionType
OP = mybir.AluOpType
AX = mybir.AxisListType.X

L = 4096
D = 1024
DIN = 2048
DIP = 4256
H = 32
PH = 64
NS = 64
Q = 128
NCH = 32
T = 512
NT = 8
NCT = 34
E = 8
F = 4096
EPS_RMS = 1.1920929e-07
EPS_LN = 1e-5
NW = NCT * 128
DBGY = False


_SEM_REG = []


class Sem:
    def __init__(self, nc, ctx, name):
        self.h = ctx.enter_context(nc.semaphore(name))
        self.n = 0
        _SEM_REG.append(self.h)

    def i(self, instr, amt=1):
        instr.then_inc(self.h, amt)
        self.n += amt
        return instr


def build(dbg=False, phases=("a0", "a1", "a2", "b", "c")):
    nc = bass.Bass("TRN2", target_bir_lowering=False)

    def dram(name, shape, dt=F32, kind="ExternalInput"):
        return nc.dram_tensor(name, shape, dt, kind=kind).ap()

    sk = "ExternalOutput" if dbg else "Internal"

    hb = {
        "u": dram("u", [L, D]),
        "win": dram("w_in", [DIP, D]),
        "cw": dram("conv_w", [DIP, 4]),
        "cb": dram("conv_b", [DIP]),
        "alog": dram("a_log", [H]),
        "dpar": dram("dparam", [DIN]),
        "dtb": dram("dt_bias", [H]),
        "nw": dram("norm_w", [DIN]),
        "wout": dram("w_out", [D, DIN]),
        "gw": dram("gate_w", [D, E]),
        "fc": dram("fc_w", [E, D, F]),
        "pj": dram("proj_w", [E, F, D]),
        "idf": dram("idn_f", [128, 128]),
        "idb": dram("idn_b", [128, 128]),
        "mask": dram("maskb", [128, 128]),
        "iota8": dram("iota8", [128, 8]),
        "msel": dram("msel", [16, E * 128]),
        "y": dram("y", [L, D], F32, "ExternalOutput"),
        "S_z": dram("S_z", [DIN, L], F32, sk),
        "S_xt": dram("S_xt", [L, DIN], F32, sk),
        "S_B": dram("S_B", [NS, L], BF16, sk),
        "S_Bt": dram("S_Bt", [L, NS], BF16, sk),
        "S_C": dram("S_C", [NS, L], BF16, sk),
        "S_sfm": dram("S_sfm", [H, L], F32, sk),
        "S_st": dram("S_st", [L, H], F32, sk),
        "S_wt": dram("S_wt", [L, H], F32, sk),
        "S_dtt": dram("S_dtt", [L, H], F32, sk),
        "S_wfm": dram("S_wfm", [H, L], F32, sk),
        "S_dfm": dram("S_dfm", [H, L], F32, sk),
        "S_shl": dram("S_shl", [2, NCH * H * Q], BF16, sk),
        "S_es": dram("S_es", [1, NCH * H * Q], BF16, sk),
        "S_esl": dram("S_esl", [NCH, H], F32, sk),
        "S_u": dram("S_u", [D, L], F32, sk),
        "S_g": dram("S_g", [DIN, L], BF16, sk),
        "S_x1f": dram("S_x1f", [D, L], F32, sk),
        "S_x1b": dram("S_x1b", [D, L], BF16, sk),
        "S_mhl": dram("S_mhl", [16, L], BF16, sk),
        "S_acc": dram("S_acc", [D, L], F32, sk),
        "S_winT": dram("S_winT", [128, 8 * NW], BF16, sk),
        "S_winT2": dram("S_winT2", [128, 8 * NW], BF16, sk),
        "S_yf": dram("S_yf", [L, DIN], F32, sk),
        "S_hst": dram("S_hst", [NS, H * PH], F32, sk),
    }

    ost = contextlib.ExitStack()
    C_ = ost.enter_context
    with ost:
        cst = {
            "idf": C_(nc.sbuf_tensor("c_idf", [128, 128], F32)),
            "idb": C_(nc.sbuf_tensor("c_idb", [128, 128], BF16)),
            "maskb": C_(nc.sbuf_tensor("c_maskb", [128, 128], F32)),
            "iota8": C_(nc.sbuf_tensor("c_iota8", [128, 8], F32)),
            "msel": C_(nc.sbuf_tensor("c_msel", [16, E * 128], BF16)),
            "ones_b": C_(nc.sbuf_tensor("c_onesb", [2, 128], BF16)),
            "ones_f": C_(nc.sbuf_tensor("c_onesf", [128, 1], F32)),
            "onesr_f": C_(nc.sbuf_tensor("c_onesrf", [1, 128], F32)),
            "eps_rms": C_(nc.sbuf_tensor("c_epsr", [128, 1], F32)),
            "eps_ln": C_(nc.sbuf_tensor("c_epsl", [1, 1], F32)),
            "cw_col": C_(nc.sbuf_tensor("c_cw", [128, NCT * 4], F32)),
            "cb_col": C_(nc.sbuf_tensor("c_cb", [128, NCT], F32)),
            "dtb_col": C_(nc.sbuf_tensor("c_dtbc", [H, 1], F32)),
            "cbdt_col": C_(nc.sbuf_tensor("c_cbdt", [H, 1], F32)),
            "alog_col": C_(nc.sbuf_tensor("c_alog", [H, 1], F32)),
            "nw_col": C_(nc.sbuf_tensor("c_nw", [128, 16], F32)),
            "gw_f": C_(nc.sbuf_tensor("c_gw", [128, 8 * E], F32)),
            "zeros32": C_(nc.sbuf_tensor("c_z32", [H, 128], F32)),
        }
        _prelude(nc, ost, hb, cst)
        if "a0" in phases:
            _phase_a0(nc, ost, hb, cst)
        if "a1" in phases:
            _phase_a1(nc, ost, hb, cst)
        if "a2" in phases:
            _phase_a2(nc, ost, hb, cst)
        if "b" in phases:
            _phase_b(nc, ost, hb, cst)
        if "c" in phases:
            _phase_c(nc, ost, hb, cst)

    nc.finalize()
    return nc


def _clear_sems(nc, tag):
    with nc.Block(tag + "_clr"):
        for h in _SEM_REG:
            nc.sync.sem_clear(h)
    _SEM_REG.clear()


def _prelude(nc, ost, hb, cst):
    st = contextlib.ExitStack()
    C_ = st.enter_context
    ld = Sem(nc, ost, "pre_ld")
    dn = Sem(nc, ost, "pre_dn")
    pemm = Sem(nc, ost, "pre_pe")
    _clear_sems(nc, "pre")
    with st, nc.Block("prelude"):
        sy, gp, ve, te = nc.sync, nc.gpsimd, nc.vector, nc.tensor
        ld.i(sy.dma_start(cst["idf"][:], hb["idf"][:]), 16)
        ld.i(sy.dma_start(cst["maskb"][:], hb["mask"][:]), 16)
        ld.i(sy.dma_start(cst["iota8"][:], hb["iota8"][:]), 16)
        ld.i(sy.dma_start(cst["cw_col"][:, 0:132].rearrange("p (c k) -> p c k", k=4),
                          hb["cw"][0:4224].rearrange("(c p) k -> p c k", p=128)),
             16)
        ld.i(sy.dma_start(cst["cw_col"][0:32, 132:136], hb["cw"][4224:4256]), 16)
        with nc.allow_non_contiguous_dma(reason="tiny per-partition param loads"):
            ld.i(sy.dma_start(cst["cb_col"][:, 0:33],
                              hb["cb"][0:4224].rearrange("(c p) -> p c", p=128)), 16)
        ld.i(sy.dma_start(cst["cb_col"][0:32, 33:34],
                          hb["cb"][4224:4256].rearrange("(p o) -> p o", o=1)), 16)
        ld.i(sy.dma_start(cst["dtb_col"][:],
                          hb["dtb"].rearrange("(h o) -> h o", o=1)), 16)
        ld.i(sy.dma_start(cst["alog_col"][:],
                          hb["alog"].rearrange("(h o) -> h o", o=1)), 16)
        with nc.allow_non_contiguous_dma(reason="tiny"):
            ld.i(sy.dma_start(cst["nw_col"][:],
                              hb["nw"].rearrange("(c p) -> p c", p=128)), 16)
        ld.i(sy.dma_start(cst["gw_f"][:].rearrange("p (d e) -> p d e", e=E),
                          hb["gw"].rearrange("(d p) e -> p d e", p=128)), 16)
        ld.i(gp.dma_start(cst["idb"][:], hb["idb"][:]), 16)
        ld.i(gp.dma_start(cst["msel"][:], hb["msel"][:]), 16)
        ve.memset(cst["ones_b"][:], 1.0)
        ve.memset(cst["ones_f"][:], 1.0)
        ve.memset(cst["onesr_f"][:], 1.0)
        ve.memset(cst["eps_rms"][:], EPS_RMS)
        ve.memset(cst["eps_ln"][:], EPS_LN)
        ve.memset(cst["zeros32"][:], 0.0)
        ve.wait_ge(ld.h, ld.n)
        dn.i(ve.tensor_tensor(cst["cbdt_col"][:], cst["cb_col"][0:H, 33:34],
                              cst["dtb_col"][:], OP.add))


def _phase_a0(nc, ost, hb, cst):
    st = contextlib.ExitStack()
    C_ = st.enter_context
    ld = Sem(nc, ost, "a0_ld")
    lu = Sem(nc, ost, "a0_lu")
    sg = Sem(nc, ost, "a0_sg")
    pe = Sem(nc, ost, "a0_pe")
    s_bte = Sem(nc, ost, "a0_bte")
    s_tre = Sem(nc, ost, "a0_tre")
    s_ze = Sem(nc, ost, "a0_ze")
    s_tap = Sem(nc, ost, "a0_tap")
    s_dtap = Sem(nc, ost, "a0_dtap")
    s_hal = Sem(nc, ost, "a0_hal")
    dv = Sem(nc, ost, "a0_dv")
    ac = Sem(nc, ost, "a0_ac")
    so = Sem(nc, ost, "a0_so")
    utr = Sem(nc, ost, "a0_utr")
    _clear_sems(nc, "a0")
    with st, nc.Block("a0"):
        sy, sc, gp, te, ve, se = (nc.sync, nc.scalar, nc.gpsimd, nc.tensor,
                                  nc.vector, nc.scalar)
        win_T = C_(nc.sbuf_tensor("a0_winT", [128, 8 * NW], BF16))
        u_tok = [C_(nc.sbuf_tensor("a0_utok0", [128, 4 * D], F32))]
        u_fm = C_(nc.sbuf_tensor("a0_ufm", [128, 8 * T], F32))
        u_fmb = C_(nc.sbuf_tensor("a0_ufmb", [128, 8 * T], BF16))
        zxw = [C_(nc.sbuf_tensor(f"a0_zxw{i}", [128, 515], F32)) for i in range(4)]
        conv = [C_(nc.sbuf_tensor(f"a0_conv{i}", [128, T], F32)) for i in range(4)]
        xcv = C_(nc.sbuf_tensor("a0_xcv", [128, 16 * T], F32))
        xst = [C_(nc.sbuf_tensor(f"a0_xst{i}", [128, DIN], F32)) for i in range(2)]
        halo = C_(nc.sbuf_tensor("a0_halo", [128, NCT * 3], F32))
        bc_b = C_(nc.sbuf_tensor("a0_bcb", [128, T], BF16))
        bt_st = C_(nc.sbuf_tensor("a0_btst", [128, 4 * 64], BF16))
        dt_f = C_(nc.sbuf_tensor("a0_dt", [H, T], F32))
        tmp1 = C_(nc.sbuf_tensor("a0_tmp1", [H, T], F32))
        tmp2 = C_(nc.sbuf_tensor("a0_tmp2", [H, T], F32))
        s_fm = C_(nc.sbuf_tensor("a0_sfm", [H, T], F32))
        es_fm = C_(nc.sbuf_tensor("a0_esfm", [H, T], F32))
        es_b = C_(nc.sbuf_tensor("a0_esb", [H, T], BF16))
        w_fm = C_(nc.sbuf_tensor("a0_wfm", [H, T], F32))
        shi_b = C_(nc.sbuf_tensor("a0_shib", [H, T], BF16))
        tok_st = C_(nc.sbuf_tensor("a0_tokst", [128, 12 * 32], F32))
        shi_f = C_(nc.sbuf_tensor("a0_shif", [H, T], F32))
        slo_f = C_(nc.sbuf_tensor("a0_slof", [H, T], F32))

        PSF = C_(nc.psum_tensor("a0_psf", [128, 4 * 512], F32))
        PSB = C_(nc.psum_tensor("a0_psb", [128, 2 * 1024], BF16))
        pz = [PSF[:, 0:512], PSF[:, 512:1024]]
        ptr = [PSF[:, 1024:1536], PSF[:, 1536:2048]]
        pbt = [PSB[:, 0:1024], PSB[:, 1024:2048]]

        with nc.sbuf_tensor("a0_wincm", [128, 4 * D], BF16) as win_cm:
            lu.i(sy.dma_start(
                u_tok[0].ap().rearrange("p (lt d) -> p lt d", lt=4),
                hb["u"].rearrange("(t lt p) d -> t p lt d", p=128, lt=4)[0]), 16)
            ld_u = [lu.n]
            nbth = [0]
            groups = [(0, 4), (4, 4), (8, 4), (12, 4), (16, 4), (20, 4), (24, 4), (28, 4), (32, 2)]
            wld = []
            for gi, (c0, ncg) in enumerate(groups):
                if gi > 0:
                    gp.wait_ge(pe.h, pe.n)
                nfull = min(ncg, 33 - c0) if c0 < 33 else 0
                if nfull > 0:
                    ld.i(gp.dma_start(
                        win_cm[:, 0:nfull * D].rearrange("p (c d) -> p c d", d=D),
                        hb["win"][c0 * 128:(c0 + nfull) * 128].rearrange(
                            "(c p) d -> p c d", p=128)), 16)
                if c0 + ncg > 33:
                    ld.i(gp.dma_start(win_cm[0:32, nfull * D:(nfull + 1) * D],
                                      hb["win"][4224:4256]), 16)
                wld.append(ld.n)
                te.wait_ge(ld.h, wld[-1])
                for ci in range(ncg):
                    ct = c0 + ci
                    npart = 128 if ct < 33 else 32
                    for dt in range(8):
                        nbt = nbth[0]
                        if nbt >= 2:
                            te.wait_ge(s_bte.h, nbt - 1)
                        p = pbt[nbt % 2][0:128, 0:npart]
                        pe.i(te.transpose(
                            p, win_cm[0:npart, ci * D + dt * 128:
                                      ci * D + (dt + 1) * 128],
                            cst["idb"][0:npart, 0:npart]))
                        ve.wait_ge(pe.h, pe.n)
                        s_bte.i(ve.tensor_copy(
                            win_T[:, dt * NW + ct * 128:dt * NW + ct * 128 + npart],
                            p))
                        nbth[0] += 1

        if DBGY:
            sc.wait_ge(s_bte.h, 272)
            so.i(sc.dma_start(hb["S_winT"][:], win_T[:]), 16)
        ve.memset(halo[:], 0.0)
        guards = {}
        xst_store = {}
        ntr = 0
        ze_g = 0

        for t in range(NT):
            buf = 0
            te.wait_ge(lu.h, ld_u[t])
            if t >= 1:
                ve.wait_ge(so.h, ufm_store[0])
            for lt in range(4):
                for dt in range(8):
                    if ntr >= 2:
                        te.wait_ge(s_tre.h, ntr - 1)
                    p = ptr[ntr % 2][:, 0:128]
                    pe.i(te.transpose(
                        p, u_tok[buf][:, lt * D + dt * 128:lt * D + (dt + 1) * 128],
                        cst["idf"][:]))
                    ve.wait_ge(pe.h, pe.n)
                    s_tre.i(ve.tensor_copy(
                        u_fm[:, dt * T + lt * 128:dt * T + (lt + 1) * 128], p))
                    ntr += 1
            utr.i(te.nop())
            if t + 1 < NT:
                sy.wait_ge(utr.h, t + 1)
                lu.i(sy.dma_start(
                    u_tok[0].ap().rearrange("p (lt d) -> p lt d", lt=4),
                    hb["u"].rearrange("(t lt p) d -> t p lt d",
                                      p=128, lt=4)[t + 1]), 16)
                ld_u.append(lu.n)
            ve.wait_ge(s_tre.h, ntr)
            dv.i(ve.tensor_copy(u_fmb[:], u_fm[:]))
            ufm_tick = dv.n
            sc.wait_ge(dv.h, ufm_tick)
            so.i(sc.dma_start(
                hb["S_u"].rearrange("(dt p) l -> p dt l", p=128)[
                    :, :, t * T:(t + 1) * T],
                u_fm.ap().rearrange("p (dt l) -> p dt l", dt=8)), 16)
            ufm_store = [so.n]

            te.wait_ge(dv.h, ufm_tick)
            te.wait_ge(s_bte.h, 272)
            if t > 0:
                se.wait_ge(pe.h, x_trdone)
            for ct in range(NCT):
                g = ze_g
                npart = 128 if ct < 33 else 32
                if g >= 2:
                    te.wait_ge(s_ze.h, g - 1)
                for dt in range(8):
                    mm = te.matmul(
                        pz[g % 2][0:npart, :],
                        win_T[:, dt * NW + ct * 128:dt * NW + ct * 128 + npart],
                        u_fmb[:, dt * T:(dt + 1) * T],
                        start=(dt == 0), stop=(dt == 7))
                    if dt == 7:
                        pe.i(mm)
                wbuf = zxw[g % 4]
                cbuf = xcv[:, (ct - 16) * T:(ct - 15) * T] if 16 <= ct < 32 \
                    else conv[g % 4]
                se.wait_ge(pe.h, pe.n)
                if g >= 4:
                    se.wait_ge(s_dtap.h, g - 3)
                    se.wait_ge(s_tap.h, g - 3)
                    se.wait_ge(s_hal.h, g - 3)
                if g >= 4 and (g - 4) in guards:
                    gh, gv = guards[g - 4]
                    se.wait_ge(gh, gv)
                s_ze.i(se.copy(wbuf[:, 3:515], pz[g % 2][:]))
                ve.wait_ge(s_ze.h, g + 1)
                dv.i(ve.tensor_copy(wbuf[:, 0:3], halo[:, ct * 3:ct * 3 + 3]))
                bias_ap = cst["cbdt_col"][:] if ct == 33 else \
                    cst["cb_col"][0:npart, ct:ct + 1]
                ac.i(se.activation(cbuf[0:npart, :], wbuf[0:npart, 3:515],
                                   AF.Identity,
                                   scale=cst["cw_col"][0:npart,
                                                       ct * 4 + 3:ct * 4 + 4],
                                   bias=bias_ap))
                s_hal.i(ve.tensor_copy(halo[:, ct * 3:ct * 3 + 3],
                                       wbuf[:, 512:515]))
                ve.wait_ge(ac.h, ac.n)
                s_dtap.i(ve.scalar_tensor_tensor(
                    cbuf[0:npart, :], wbuf[0:npart, 2:514],
                    cst["cw_col"][0:npart, ct * 4 + 2:ct * 4 + 3],
                    cbuf[0:npart, :], OP.mult, OP.add))
                ve.scalar_tensor_tensor(
                    cbuf[0:npart, :], wbuf[0:npart, 1:513],
                    cst["cw_col"][0:npart, ct * 4 + 1:ct * 4 + 2],
                    cbuf[0:npart, :], OP.mult, OP.add)
                s_tap.i(ve.scalar_tensor_tensor(
                    cbuf[0:npart, :], wbuf[0:npart, 0:512],
                    cst["cw_col"][0:npart, ct * 4 + 0:ct * 4 + 1],
                    cbuf[0:npart, :], OP.mult, OP.add))
                tap_n = s_tap.n

                if ct < 16:
                    sc.wait_ge(s_tap.h, tap_n)
                    so.i(sc.dma_start(hb["S_z"][ct * 128:(ct + 1) * 128,
                                                t * T:(t + 1) * T], cbuf[:]), 16)
                    guards[g] = (so.h, so.n)
                elif ct < 32:
                    if ct == 31:
                        x_tapn = s_tap.n
                elif ct == 32:
                    ve.wait_ge(s_tap.h, tap_n)
                    if t >= 1:
                        ve.wait_ge(so.h, bc_store[0])
                    dv.i(ve.tensor_copy(bc_b[:], cbuf[:]))
                    guards[g] = (dv.h, dv.n)
                    bcb_tick = dv.n
                    sc.wait_ge(dv.h, bcb_tick)
                    so.i(sc.dma_start(hb["S_B"][:, t * T:(t + 1) * T],
                                      bc_b[0:64, :]), 16)
                    so.i(sc.dma_start(hb["S_C"][:, t * T:(t + 1) * T],
                                      bc_b[64:128, :]), 16)
                    te.wait_ge(dv.h, bcb_tick)
                    for lt in range(4):
                        nbt = nbth[0]
                        if nbt >= 2:
                            te.wait_ge(s_bte.h, nbt - 1)
                        p = pbt[nbt % 2][0:128, 0:64]
                        pe.i(te.transpose(p, bc_b[0:64, lt * 128:(lt + 1) * 128],
                                          cst["idb"][0:64, 0:64]))
                        ve.wait_ge(pe.h, pe.n)
                        s_bte.i(ve.tensor_copy(bt_st[:, lt * 64:(lt + 1) * 64], p))
                        nbth[0] += 1
                    sc.wait_ge(s_bte.h, nbth[0])
                    so.i(sc.dma_start(
                        hb["S_Bt"].rearrange("(tt lt p) n -> tt p lt n",
                                             p=128, lt=4)[t],
                        bt_st.ap().rearrange("p (lt n) -> p lt n", lt=4)), 16)
                    bc_store = [so.n]
                else:
                    se.wait_ge(s_tap.h, tap_n)
                    if t >= 1:
                        se.wait_ge(so.h, dtb_store[0])
                        se.wait_ge(sg.h, dtb_sg[0])
                    ac.i(se.activation(tmp1[:], cbuf[0:H, :], AF.Abs))
                    ac.i(se.activation(tmp1[:], tmp1[:], AF.Exp, scale=-1.0))
                    ac.i(se.activation(tmp1[:], tmp1[:], AF.Ln,
                                       bias=cst["ones_f"][0:H, :]))
                    ac.i(se.activation(tmp2[:], cbuf[0:H, :], AF.Relu))
                    guards[g] = (ac.h, ac.n)
                    ve.wait_ge(ac.h, ac.n)
                    dv.i(ve.tensor_tensor(dt_f[:], tmp1[:], tmp2[:], OP.add))
                    dv.i(ve.tensor_scalar(tmp1[:], dt_f[:], cst["alog_col"][:],
                                          None, OP.mult))
                    for cc in range(4):
                        dv.i(ve.tensor_tensor_scan(
                            s_fm[:, cc * 128:(cc + 1) * 128],
                            tmp1[:, cc * 128:(cc + 1) * 128], cst["zeros32"][:],
                            0.0, OP.add, OP.add))
                    se.wait_ge(dv.h, dv.n)
                    ac.i(se.activation(es_fm[:], s_fm[:], AF.Exp))
                    for cc in range(4):
                        sl = s_fm[:, cc * 128 + 127:cc * 128 + 128]
                        dv.i(ve.tensor_scalar(tmp2[:, cc * 128:(cc + 1) * 128],
                                              s_fm[:, cc * 128:(cc + 1) * 128],
                                              -1.0, sl, OP.mult, OP.add))
                    se.wait_ge(dv.h, dv.n)
                    ac.i(se.activation(tmp2[:], tmp2[:], AF.Exp))
                    ve.wait_ge(ac.h, ac.n)
                    dv.i(ve.tensor_tensor(w_fm[:], tmp2[:], dt_f[:], OP.mult))
                    dv.i(ve.tensor_copy(shi_b[:], s_fm[:]))
                    dv.i(ve.tensor_copy(shi_f[:], shi_b[:]))
                    dv.i(ve.tensor_tensor(slo_f[:], s_fm[:], shi_f[:],
                                          OP.subtract))
                    dv.i(ve.tensor_copy(es_b[:], es_fm[:]))
                    dt_done = dv.n
                    sc.wait_ge(dv.h, dt_done)
                    for cc in range(4):
                        c = t * 4 + cc
                        with nc.allow_non_contiguous_dma(reason="tiny esl row"):
                            so.i(sc.dma_start(
                                hb["S_esl"][c:c + 1, :].rearrange("a h -> h a"),
                                es_fm[:, cc * 128 + 127:cc * 128 + 128]), 16)
                    shl_r = hb["S_shl"].rearrange("r (c h l) -> r c h l", h=H, l=Q)
                    so.i(sc.dma_start(
                        shl_r[0, t * 4:(t + 1) * 4].rearrange("c h l -> h c l"),
                        shi_b.ap().rearrange("h (c l) -> h c l", c=4)), 16)
                    gp.wait_ge(dv.h, dt_done)
                    sg.i(gp.dma_start(
                        shl_r[1, t * 4:(t + 1) * 4].rearrange("c h l -> h c l"),
                        slo_f.ap().rearrange("h (c l) -> h c l", c=4)), 16)
                    es_r = hb["S_es"].rearrange("r (c h l) -> r c h l", h=H, l=Q)
                    so.i(sc.dma_start(
                        es_r[0, t * 4:(t + 1) * 4].rearrange("c h l -> h c l"),
                        es_b.ap().rearrange("h (c l) -> h c l", c=4)), 16)
                    so.i(sc.dma_start(hb["S_sfm"][:, t * T:(t + 1) * T], s_fm[:]),
                         16)
                    so.i(sc.dma_start(hb["S_wfm"][:, t * T:(t + 1) * T], w_fm[:]),
                         16)
                    so.i(sc.dma_start(hb["S_dfm"][:, t * T:(t + 1) * T], dt_f[:]),
                         16)
                    te.wait_ge(dv.h, dt_done)
                    for cc in range(4):
                        for j, src in enumerate((s_fm, w_fm, dt_f)):
                            if ntr >= 2:
                                te.wait_ge(s_tre.h, ntr - 1)
                            p = ptr[ntr % 2][:, 0:H]
                            pe.i(te.transpose(
                                p, src[0:H, cc * 128:(cc + 1) * 128],
                                cst["idf"][0:H, 0:H]))
                            ve.wait_ge(pe.h, pe.n)
                            s_tre.i(ve.tensor_copy(
                                tok_st[:, (cc * 3 + j) * 32:(cc * 3 + j + 1) * 32],
                                p))
                            ntr += 1
                    sc.wait_ge(s_tre.h, ntr)
                    trr = tok_st.ap().rearrange("p (cc j n) -> p cc j n",
                                                cc=4, j=3)
                    for cc in range(4):
                        cgl = t * 4 + cc
                        so.i(sc.dma_start(hb["S_st"][cgl * Q:(cgl + 1) * Q, :],
                                          trr[:, cc, 0]), 16)
                        so.i(sc.dma_start(hb["S_wt"][cgl * Q:(cgl + 1) * Q, :],
                                          trr[:, cc, 1]), 16)
                        so.i(sc.dma_start(hb["S_dtt"][cgl * Q:(cgl + 1) * Q, :],
                                          trr[:, cc, 2]), 16)
                    dtb_store = [so.n]
                    dtb_sg = [sg.n]
                ze_g += 1

            te.wait_ge(s_tap.h, x_tapn)
            for lt in range(4):
                bx = lt % 2
                prev = xst_store.get((t, lt - 2)) if lt >= 2 else \
                    xst_store.get((t - 1, lt + 2))
                if prev:
                    ve.wait_ge(so.h, prev)
                for cx in range(16):
                    if ntr >= 2:
                        te.wait_ge(s_tre.h, ntr - 1)
                    p = ptr[ntr % 2][:, 0:128]
                    pe.i(te.transpose(
                        p, xcv[:, cx * T + lt * 128:cx * T + (lt + 1) * 128],
                        cst["idf"][:]))
                    ve.wait_ge(pe.h, pe.n)
                    s_tre.i(ve.tensor_copy(xst[bx][:, cx * 128:(cx + 1) * 128],
                                           p))
                    ntr += 1
                sc.wait_ge(s_tre.h, ntr)
                so.i(sc.dma_start(
                    hb["S_xt"][t * T + lt * 128:t * T + (lt + 1) * 128, :],
                    xst[bx][:]), 16)
                xst_store[(t, lt)] = so.n
            x_trdone = pe.n

        if DBGY:
            sc.wait_ge(s_tap.h, s_tap.n)
            so.i(sc.dma_start(hb["S_winT2"][:], win_T[:]), 16)
        sy.wait_ge(so.h, so.n)
        sy.wait_ge(sg.h, sg.n)


def _phase_a1(nc, ost, hb, cst):
    st = contextlib.ExitStack()
    C_ = st.enter_context
    ld2 = [Sem(nc, ost, "a1_ld0"), Sem(nc, ost, "a1_ld1")]
    pe = Sem(nc, ost, "a1_pe")
    dv = Sem(nc, ost, "a1_dv")
    ac = Sem(nc, ost, "a1_ac")
    so = Sem(nc, ost, "a1_so")
    s67 = Sem(nc, ost, "a1_s67")
    s_ye = Sem(nc, ost, "a1_ye")
    s_hup = Sem(nc, ost, "a1_hup")
    s_fin = Sem(nc, ost, "a1_fin")
    s_pl = Sem(nc, ost, "a1_pl")
    _clear_sems(nc, "a1")
    with st, nc.Block("a1"):
        sy, sc, gp, te, ve, se = (nc.sync, nc.scalar, nc.gpsimd, nc.tensor,
                                  nc.vector, nc.scalar)
        xt_f2 = [C_(nc.sbuf_tensor(f"a1_xtf2{i}", [128, DIN], F32))
                 for i in range(2)]
        z_f = [C_(nc.sbuf_tensor(f"a1_zf{i}", [128, 16 * Q], F32))
               for i in range(2)]
        B_fm = [C_(nc.sbuf_tensor(f"a1_bfm{i}", [NS, Q], BF16)) for i in range(2)]
        Cfm = [C_(nc.sbuf_tensor(f"a1_cfm{i}", [NS, Q], BF16)) for i in range(2)]
        st_c2 = [C_(nc.sbuf_tensor(f"a1_stc2{i}", [128, H], F32))
                 for i in range(2)]
        wt_c2 = [C_(nc.sbuf_tensor(f"a1_wtc2{i}", [128, H], F32))
                 for i in range(2)]
        dtt_c2 = [C_(nc.sbuf_tensor(f"a1_dttc2{i}", [128, H], F32))
                  for i in range(2)]
        shl_c = [C_(nc.sbuf_tensor(f"a1_shl{i}", [2, H * Q], BF16))
                 for i in range(2)]
        es_c = [C_(nc.sbuf_tensor(f"a1_es{i}", [1, H * Q], BF16))
                for i in range(2)]
        esl_c = [C_(nc.sbuf_tensor(f"a1_esl{i}", [1, H], F32)) for i in range(2)]
        xt_b = C_(nc.sbuf_tensor("a1_xtb", [128, DIN], BF16))
        xdt_b = C_(nc.sbuf_tensor("a1_xdtb", [128, DIN], BF16))
        bw_all = C_(nc.sbuf_tensor("a1_bwall", [128, DIN], BF16))
        btm_f = C_(nc.sbuf_tensor("a1_btmf", [128, NS], F32))
        Btm = [C_(nc.sbuf_tensor(f"a1_btm{i}", [128, NS], BF16)) for i in range(2)]
        eslb = C_(nc.sbuf_tensor("a1_eslb", [NS, H], F32))
        cbt = C_(nc.sbuf_tensor("a1_cbt", [128, Q], F32))
        Lx = [C_(nc.sbuf_tensor(f"a1_lx{i}", [128, Q], F32)) for i in range(3)]
        Lw = [C_(nc.sbuf_tensor(f"a1_lw{i}", [128, Q], F32)) for i in range(3)]
        mt = [C_(nc.sbuf_tensor(f"a1_mt{i}", [128, Q], BF16)) for i in range(3)]
        cl_all = C_(nc.sbuf_tensor("a1_clall", [NS, H * Q], BF16))
        cfm_f = C_(nc.sbuf_tensor("a1_cfmf", [NS, Q], F32))
        hst_f = C_(nc.sbuf_tensor("a1_hstf", [NS, H * PH], F32))
        hst_b = C_(nc.sbuf_tensor("a1_hstb", [NS, H * PH], BF16))
        xd = C_(nc.sbuf_tensor("a1_xd", [128, DIN], F32))
        y_full = C_(nc.sbuf_tensor("a1_yf", [128, DIN], F32))
        y2j = C_(nc.sbuf_tensor("a1_y2j", [128, DIN], F32))
        ssq = C_(nc.sbuf_tensor("a1_ssq", [128, 1], F32))
        rstd = C_(nc.sbuf_tensor("a1_rstd", [128, 1], F32))
        zs = [C_(nc.sbuf_tensor(f"a1_zs{i}", [128, DIN], F32))
              for i in range(2)]
        gated = C_(nc.sbuf_tensor("a1_gated", [128, DIN], BF16))

        PS = C_(nc.psum_tensor("a1_ps", [128, 8 * 512], F32))
        py = [PS[:, i * 512:(i + 1) * 512] for i in range(4)]
        pmix = PS[:, 4 * 512:5 * 512]
        pdh = PS[:, 5 * 512:6 * 512]
        p67 = [PS[:, 6 * 512:7 * 512], PS[:, 7 * 512:8 * 512]]

        D_tb = C_(nc.sbuf_tensor("a1_dtb2", [128, DIN], F32))
        drow = C_(nc.sbuf_tensor("a1_drow", [1, DIN], F32))
        dhl = C_(nc.sbuf_tensor("a1_dhl", [2, DIN], BF16))
        dhf = C_(nc.sbuf_tensor("a1_dhf", [1, DIN], F32))
        dlf = C_(nc.sbuf_tensor("a1_dlf", [1, DIN], F32))
        ve.memset(hst_f[:], 0.0)
        dv.i(ve.memset(hst_b[:], 0.0))
        ld2[0].i(sy.dma_start(drow[:], hb["dpar"].rearrange("(o n) -> o n", o=1)),
                 16)
        ve.wait_ge(ld2[0].h, ld2[0].n)
        ve.tensor_copy(dhl[0:1, :], drow[:])
        ve.tensor_copy(dhf[:], dhl[0:1, :])
        dv.i(ve.tensor_tensor(dlf[:], drow[:], dhf[:], OP.subtract))
        gp.wait_ge(dv.h, dv.n)
        dv.i(gp.dma_start(dhl[1:2, :], dlf[:]), 16)
        te.wait_ge(dv.h, dv.n)
        for i in range(4):
            pe.i(te.matmul(PS[:, i * 512:(i + 1) * 512], cst["ones_b"][:],
                           dhl[:, i * 512:(i + 1) * 512], start=True, stop=True))
        ve.wait_ge(pe.h, pe.n)
        dv.i(ve.tensor_copy(D_tb[:], PS[:, 0:DIN]))

        def load_chunk(c):
            b = c % 2
            ld = ld2[b]
            ld.i(sy.dma_start(xt_f2[b][:], hb["S_xt"][c * Q:(c + 1) * Q, :]), 16)
            ld.i(sy.dma_start(
                z_f[b].ap().rearrange("p (ct l) -> p ct l", ct=16),
                hb["S_z"].rearrange("(ct p) l -> p ct l", p=128)[
                    :, :, c * Q:(c + 1) * Q]), 16)
            ld.i(sy.dma_start(B_fm[b][:], hb["S_B"][:, c * Q:(c + 1) * Q]), 16)
            ld.i(sy.dma_start(Cfm[b][:], hb["S_C"][:, c * Q:(c + 1) * Q]), 16)
            ld.i(sy.dma_start(st_c2[b][:], hb["S_st"][c * Q:(c + 1) * Q, :]), 16)
            ld.i(sy.dma_start(wt_c2[b][:], hb["S_wt"][c * Q:(c + 1) * Q, :]), 16)
            ld.i(sy.dma_start(dtt_c2[b][:], hb["S_dtt"][c * Q:(c + 1) * Q, :]),
                 16)
            ld.i(sy.dma_start(shl_c[b][:],
                              hb["S_shl"][:, c * H * Q:(c + 1) * H * Q]), 16)
            ld.i(sy.dma_start(es_c[b][:],
                              hb["S_es"][:, c * H * Q:(c + 1) * H * Q]), 16)
            ld.i(sy.dma_start(Btm[b][:], hb["S_Bt"][c * Q:(c + 1) * Q, :]), 16)
            ld.i(sy.dma_start(esl_c[b][:], hb["S_esl"][c:c + 1, :]), 16)
            return ld.n

        n67 = 0

        def rot67():
            nonlocal n67
            k = n67
            if k >= 2:
                te.wait_ge(s67.h, k - 1)
            slot = p67[k % 2]
            n67 += 1
            return slot

        ld_tick = [load_chunk(0)]
        g_store = [0]

        for c in range(NCH):
            b = c % 2
            if c + 1 < NCH:
                if c >= 1:
                    sy.wait_ge(s_fin.h, 3 * c)
                ld_tick.append(load_chunk(c + 1))

            te.wait_ge(ld2[b].h, ld_tick[c])
            if c >= 1:
                te.wait_ge(s_fin.h, 3 * c)

            ve.wait_ge(ld2[b].h, ld_tick[c])
            dv.i(ve.tensor_copy(xt_b[:], xt_f2[b][:]))
            dv.i(ve.tensor_tensor(xd[:], xt_f2[b][:], D_tb[:], OP.mult))
            dv.i(ve.tensor_tensor(
                xdt_b[:].rearrange("p (h d) -> p h d", d=PH),
                xt_f2[b][:].rearrange("p (h d) -> p h d", d=PH),
                dtt_c2[b][:].unsqueeze(2).broadcast_to([128, H, PH]),
                OP.mult))
            dv.i(ve.tensor_copy(btm_f[:], Btm[b][:]))
            dv.i(ve.tensor_tensor(
                bw_all[:].rearrange("p (h n) -> p h n", n=NS),
                btm_f[:].unsqueeze(1).broadcast_to([128, H, NS]),
                wt_c2[b][:].unsqueeze(2).broadcast_to([128, H, NS]),
                OP.mult))
            xtb_tick = dv.n
            se.wait_ge(ld2[b].h, ld_tick[c])
            if c >= 1:
                se.wait_ge(s_fin.h, 3 * c)
            ac.i(se.activation(zs[b][:], z_f[b][:], AF.Tanh, scale=0.5))
            ve.wait_ge(ac.h, ac.n)
            dv.i(ve.tensor_scalar(zs[b][:], zs[b][:], 0.5, 0.5, OP.mult,
                                  OP.add))
            dv.i(ve.tensor_tensor(zs[b][:], zs[b][:], z_f[b][:], OP.mult))
            zs_tick = dv.n

            if c >= 1:
                te.wait_ge(s_hup.h, c * 4)
            pe.i(te.matmul(pmix[0:128, 0:128], B_fm[b][:], Cfm[b][:],
                           start=True, stop=True))
            pe.i(te.matmul(pmix[0:NS, 128:128 + H], cst["onesr_f"][0:1, 0:NS],
                           esl_c[b][:], start=True, stop=True))
            ve.wait_ge(pe.h, pe.n)
            dv.i(ve.tensor_copy(cbt[:], pmix[0:128, 0:128]))
            dv.i(ve.tensor_copy(eslb[:], pmix[0:NS, 128:128 + H]))
            pmix_evac = dv.n
            dv.i(ve.tensor_copy(cfm_f[:], Cfm[b][:]))
            for q in range(8):
                slot = rot67()
                pe.i(te.matmul(slot[0:NS, :], cst["ones_b"][0:1, 0:NS],
                               es_c[b][0:1, q * 512:(q + 1) * 512],
                               start=True, stop=True))
                ve.wait_ge(pe.h, pe.n)
                s67.i(ve.tensor_tensor(
                    cl_all[:, q * 512:(q + 1) * 512]
                    .rearrange("p (h l) -> p h l", l=Q),
                    cfm_f[:].unsqueeze(1).broadcast_to([NS, 4, Q]),
                    slot[0:NS, :].rearrange("p (h l) -> p h l", l=Q), OP.mult))

            pl_base = s_pl.n
            ymm1 = {}
            ymm2 = {}
            dh_tick = {}
            exp_tick = {}
            for hh in range(H + 2):
                if hh < H:
                    h = hh
                    p_h = h % 3
                    slot = rot67()
                    pe.i(te.matmul(slot[0:128, 0:128], cst["ones_b"][:],
                                   shl_c[b][0:2, h * Q:(h + 1) * Q],
                                   start=True, stop=True))
                    bc_tick = pe.n
                    ve.wait_ge(pe.h, bc_tick)
                    if h >= 3:
                        ve.wait_ge(ac.h, exp_tick[h - 3])
                    s67.i(ve.scalar_tensor_tensor(
                        Lx[p_h][:], slot[0:128, 0:128], st_c2[b][:, h:h + 1],
                        cst["maskb"][:], OP.subtract, OP.add))
                    se.wait_ge(s67.h, s67.n)
                    if h >= 3:
                        se.wait_ge(s_pl.h, pl_base + h - 2)
                    ac.i(se.activation(Lw[p_h][:], Lx[p_h][:], AF.Exp))
                    exp_tick[h] = ac.n
                if hh >= 2:
                    g = hh - 2
                    p_g = g % 3
                    g8 = g // 8
                    ve.wait_ge(ac.h, exp_tick[g])
                    if g >= 3:
                        ve.wait_ge(pe.h, ymm1[g - 3])
                    if g == 0:
                        ve.wait_ge(dv.h, pmix_evac)
                    s_pl.i(ve.tensor_tensor(mt[p_g][:], Lw[p_g][:], cbt[:],
                                            OP.mult))
                    te.wait_ge(s_pl.h, s_pl.n)
                    if g % 8 == 0 and c >= 1:
                        te.wait_ge(s_ye.h, (c - 1) * 4 + g8 + 1)
                    pe.i(te.matmul(py[g8][:, (g % 8) * PH:(g % 8 + 1) * PH],
                                   mt[p_g][:], xdt_b[:, g * PH:(g + 1) * PH],
                                   start=(g % 8 == 0), stop=False))
                    ymm1[g] = pe.n
                    pe.i(te.matmul(py[g8][:, (g % 8) * PH:(g % 8 + 1) * PH],
                                   cl_all[:, g * Q:(g + 1) * Q],
                                   hst_b[:, g * PH:(g + 1) * PH],
                                   start=False, stop=(g % 8 == 7)))
                    ymm2[g] = pe.n
                    dhb = pdh if g8 % 2 == 0 else pmix
                    if g == 8:
                        te.wait_ge(dv.h, pmix_evac)
                    if g == 16:
                        te.wait_ge(s_hup.h, c * 4 + 1)
                    if g == 24:
                        te.wait_ge(s_hup.h, c * 4 + 2)
                    pe.i(te.matmul(dhb[0:NS, (g % 8) * PH:(g % 8 + 1) * PH],
                                   bw_all[:, g * NS:(g + 1) * NS],
                                   xt_b[:, g * PH:(g + 1) * PH],
                                   start=(g % 8 == 0), stop=True))
                    dh_tick[g] = pe.n
                    if g % 8 == 7:
                        ve.wait_ge(pe.h, ymm2[g])
                        s_ye.i(ve.tensor_tensor(
                            y_full[:, g8 * 512:(g8 + 1) * 512],
                            xd[:, g8 * 512:(g8 + 1) * 512], py[g8][:], OP.add))
                    if g in (15, 23, 31):
                        first = {15: 0, 23: 8, 31: 16}[g]
                        ve.wait_ge(pe.h, dh_tick[first + 7])
                        db = pdh if (first // 8) % 2 == 0 else pmix
                        hsl = hst_f[:, first * PH:(first + 8) * PH]
                        dv.i(ve.tensor_tensor(
                            hsl.rearrange("p (h d) -> p h d", d=PH),
                            hsl.rearrange("p (h d) -> p h d", d=PH),
                            eslb[:, first:first + 8].unsqueeze(2)
                            .broadcast_to([NS, 8, PH]), OP.mult))
                        s_hup.i(ve.tensor_tensor(hsl, hsl, db[0:NS, :], OP.add))
                    if g == 31:
                        ve.wait_ge(pe.h, dh_tick[31])
                        hsl = hst_f[:, 24 * PH:32 * PH]
                        dv.i(ve.tensor_tensor(
                            hsl.rearrange("p (h d) -> p h d", d=PH),
                            hsl.rearrange("p (h d) -> p h d", d=PH),
                            eslb[:, 24:32].unsqueeze(2)
                            .broadcast_to([NS, 8, PH]), OP.mult))
                        s_hup.i(ve.tensor_tensor(hsl, hsl, pmix[0:NS, :],
                                                 OP.add))
            ve.wait_ge(pe.h, ymm2[31])
            dv.i(ve.tensor_copy(hst_b[:], hst_f[:]))

            if DBGY:
                sc.wait_ge(s_ye.h, s_ye.n)
                sc.wait_ge(dv.h, dv.n)
                so.i(sc.dma_start(hb["S_yf"][c * Q:(c + 1) * Q, :], y_full[:]), 16)
                if c == 0:
                    so.i(sc.dma_start(hb["S_hst"][:], hst_f[:]), 16)
                ve.wait_ge(so.h, so.n)

            se.wait_ge(dv.h, dv.n)
            ac.i(se.activation(y2j[:], y_full[:], AF.Square, accum_out=ssq[:]))
            ac.i(se.activation(rstd[:], ssq[:], AF.Ln, scale=1.0 / DIN,
                               bias=cst["eps_rms"][:]))
            ac.i(se.activation(rstd[:], rstd[:], AF.Exp, scale=-0.5))
            ve.wait_ge(ac.h, ac.n)
            dv.i(ve.tensor_scalar(y_full[:], y_full[:], rstd[:], None, OP.mult))
            ysc_tick = dv.n

            te.wait_ge(dv.h, ysc_tick)
            if c >= 1:
                ve.wait_ge(so.h, g_store[0])
            for ctl in range(16):
                slot = rot67()
                pe.i(te.transpose(slot[0:128, 0:128],
                                  y_full[:, ctl * 128:(ctl + 1) * 128],
                                  cst["idf"][:]))
                ve.wait_ge(pe.h, pe.n)
                ve.wait_ge(dv.h, zs_tick)
                s67.i(ve.scalar_tensor_tensor(
                    gated[:, ctl * 128:(ctl + 1) * 128],
                    zs[b][:, ctl * 128:(ctl + 1) * 128],
                    cst["nw_col"][:, ctl:ctl + 1], slot[0:128, 0:128],
                    OP.mult, OP.mult))
            sc.wait_ge(s67.h, n67)
            so.i(sc.dma_start(
                hb["S_g"].rearrange("(ct p) l -> p ct l", p=128)[
                    :, :, c * Q:(c + 1) * Q],
                gated.ap().rearrange("p (ct l) -> p ct l", ct=16)), 16)
            g_store = [so.n]

            s_fin.i(te.nop())
            s_fin.i(ve.nop())
            s_fin.i(se.nop())

        sy.wait_ge(so.h, so.n)


def _phase_a2(nc, ost, hb, cst):
    st = contextlib.ExitStack()
    C_ = st.enter_context
    ld = Sem(nc, ost, "a2_ld")
    lw = Sem(nc, ost, "a2_lw")
    pe = Sem(nc, ost, "a2_pe")
    dv = Sem(nc, ost, "a2_dv")
    ac = Sem(nc, ost, "a2_ac")
    so = Sem(nc, ost, "a2_so")
    sg = Sem(nc, ost, "a2_sg")
    s_wte = Sem(nc, ost, "a2_wte")
    s_oe = Sem(nc, ost, "a2_oe")
    s_fin = Sem(nc, ost, "a2_fin")
    _clear_sems(nc, "a2")
    with st, nc.Block("a2"):
        sy, sc, gp, te, ve, se = (nc.sync, nc.scalar, nc.gpsimd, nc.tensor,
                                  nc.vector, nc.scalar)
        WoT = C_(nc.sbuf_tensor("a2_wot", [128, 16 * 1024], BF16))
        g_sb = [C_(nc.sbuf_tensor(f"a2_g{i}", [128, 16 * T], BF16))
                for i in range(2)]
        u_fm = [C_(nc.sbuf_tensor(f"a2_u{i}", [128, 8 * T], F32))
                for i in range(2)]
        r1 = C_(nc.sbuf_tensor("a2_r1", [128, 8 * T], F32))
        sq = C_(nc.sbuf_tensor("a2_sq", [128, 8 * T], F32))
        x1 = C_(nc.sbuf_tensor("a2_x1", [128, 8 * T], F32))
        x1b = C_(nc.sbuf_tensor("a2_x1b", [128, 8 * T], BF16))
        rows = C_(nc.sbuf_tensor("a2_rows", [1, 8 * 512], F32))
        scb = C_(nc.sbuf_tensor("a2_scb", [128, 512], F32))
        bib = C_(nc.sbuf_tensor("a2_bib", [128, 512], F32))
        lg = C_(nc.sbuf_tensor("a2_lg", [8, 512], F32))
        lgt = C_(nc.sbuf_tensor("a2_lgt", [128, 4 * 8], F32))
        lgt8 = C_(nc.sbuf_tensor("a2_lgt8", [128, 8], F32))
        top8 = C_(nc.sbuf_tensor("a2_top8", [128, 8], F32))
        idxu = C_(nc.sbuf_tensor("a2_idx", [128, 8], U32))
        iv_f = C_(nc.sbuf_tensor("a2_ivf", [128, 2], F32))
        wsc = C_(nc.sbuf_tensor("a2_wsc", [128, 4], F32))
        m_tok = C_(nc.sbuf_tensor("a2_mtok", [128, 32], F32))
        m_fm = C_(nc.sbuf_tensor("a2_mfm", [8, 512], F32))
        mhi_b = C_(nc.sbuf_tensor("a2_mhib", [8, 512], BF16))
        mhi_f = C_(nc.sbuf_tensor("a2_mhif", [8, 512], F32))
        mlo_f = C_(nc.sbuf_tensor("a2_mlof", [8, 512], F32))

        PS = C_(nc.psum_tensor("a2_ps", [128, 6 * 512], F32))
        PSB2 = C_(nc.psum_tensor("a2_psb", [128, 2 * 1024], BF16))
        po = [PS[:, 0:512], PS[:, 512:1024]]
        p_sum = PS[:, 1024:1536]
        p_sq = PS[:, 1536:2048]
        p_scb = PS[:, 2048:2560]
        p_bib = PS[:, 2560:3072]
        pbt2 = [PSB2[:, 0:1024], PSB2[:, 1024:2048]]

        with nc.sbuf_tensor("a2_wcm", [128, 4 * DIN], BF16) as wcm:
            nwt = 0
            for gi in range(2):
                if gi > 0:
                    gp.wait_ge(pe.h, pe.n)
                lw.i(gp.dma_start(
                    wcm[:].rearrange("p (c d) -> p c d", d=DIN),
                    hb["wout"][gi * 512:(gi + 1) * 512].rearrange(
                        "(c p) d -> p c d", p=128)), 16)
                te.wait_ge(lw.h, lw.n)
                for ci in range(4):
                    ot = gi * 4 + ci
                    for dint in range(16):
                        if nwt >= 2:
                            te.wait_ge(s_wte.h, nwt - 1)
                        p = pbt2[nwt % 2][0:128, 0:128]
                        pe.i(te.transpose(
                            p, wcm[:, ci * DIN + dint * 128:
                                   ci * DIN + (dint + 1) * 128],
                            cst["idb"][:]))
                        ve.wait_ge(pe.h, pe.n)
                        s_wte.i(ve.tensor_copy(
                            WoT[:, dint * 1024 + ot * 128:
                                dint * 1024 + (ot + 1) * 128], p))
                        nwt += 1

        def load_tile(t):
            b = t % 2
            ld.i(sy.dma_start(
                g_sb[b].ap().rearrange("p (ct l) -> p ct l", ct=16),
                hb["S_g"].rearrange("(ct p) l -> p ct l", p=128)[
                    :, :, t * T:(t + 1) * T]), 16)
            ld.i(sy.dma_start(
                u_fm[b].ap().rearrange("p (dt l) -> p dt l", dt=8),
                hb["S_u"].rearrange("(dt p) l -> p dt l", p=128)[
                    :, :, t * T:(t + 1) * T]), 16)
            return ld.n

        ld_tick = [load_tile(0)]
        for t in range(NT):
            b = t % 2
            if t + 1 < NT:
                if t >= 1:
                    sy.wait_ge(s_fin.h, 3 * t)
                ld_tick.append(load_tile(t + 1))
            te.wait_ge(ld.h, ld_tick[t])
            if t >= 1:
                te.wait_ge(s_fin.h, 3 * t)
            for ot in range(8):
                if ot >= 2:
                    te.wait_ge(s_oe.h, (t * 8) + ot - 1)
                for dint in range(16):
                    mm = te.matmul(po[ot % 2],
                                   WoT[:, dint * 1024 + ot * 128:
                                       dint * 1024 + (ot + 1) * 128],
                                   g_sb[b][:, dint * T:(dint + 1) * T],
                                   start=(dint == 0), stop=(dint == 15))
                    if dint == 15:
                        pe.i(mm)
                ve.wait_ge(pe.h, pe.n)
                s_oe.i(ve.tensor_tensor(r1[:, ot * T:(ot + 1) * T],
                                        u_fm[b][:, ot * T:(ot + 1) * T],
                                        po[ot % 2], OP.add))
            se.wait_ge(s_oe.h, s_oe.n)
            ac.i(se.activation(sq[:], r1[:], AF.Square))
            te.wait_ge(s_oe.h, s_oe.n)
            for ot in range(8):
                mm = te.matmul(p_sum[0:1, :], cst["ones_f"][:],
                               r1[:, ot * T:(ot + 1) * T],
                               start=(ot == 0), stop=(ot == 7))
                if ot == 7:
                    pe.i(mm)
            te.wait_ge(ac.h, ac.n)
            for ot in range(8):
                mm = te.matmul(p_sq[0:1, :], cst["ones_f"][:],
                               sq[:, ot * T:(ot + 1) * T],
                               start=(ot == 0), stop=(ot == 7))
                if ot == 7:
                    pe.i(mm)
            ve.wait_ge(pe.h, pe.n)
            dv.i(ve.tensor_scalar(rows[0:1, 0:512], p_sum[0:1, :], 1.0 / D,
                                  None, OP.mult))
            dv.i(ve.tensor_scalar(rows[0:1, 512:1024], p_sq[0:1, :], 1.0 / D,
                                  None, OP.mult))
            se.wait_ge(dv.h, dv.n)
            ac.i(se.activation(rows[0:1, 1024:1536], rows[0:1, 0:512],
                               AF.Square))
            ve.wait_ge(ac.h, ac.n)
            dv.i(ve.tensor_tensor(rows[0:1, 1536:2048], rows[0:1, 512:1024],
                                  rows[0:1, 1024:1536], OP.subtract))
            se.wait_ge(dv.h, dv.n)
            ac.i(se.activation(rows[0:1, 2048:2560], rows[0:1, 1536:2048],
                               AF.Ln, bias=cst["eps_ln"][:]))
            ac.i(se.activation(rows[0:1, 2048:2560], rows[0:1, 2048:2560],
                               AF.Exp, scale=-0.5))
            ve.wait_ge(ac.h, ac.n)
            dv.i(ve.tensor_tensor(rows[0:1, 2560:3072], rows[0:1, 0:512],
                                  rows[0:1, 2048:2560], OP.mult))
            dv.i(ve.tensor_scalar(rows[0:1, 2560:3072], rows[0:1, 2560:3072],
                                  -1.0, None, OP.mult))
            te.wait_ge(dv.h, dv.n)
            pe.i(te.matmul(p_scb[:], cst["onesr_f"][:],
                           rows[0:1, 2048:2560], start=True, stop=True))
            pe.i(te.matmul(p_bib[:], cst["onesr_f"][:],
                           rows[0:1, 2560:3072], start=True, stop=True))
            ve.wait_ge(pe.h, pe.n)
            dv.i(ve.tensor_copy(scb[:], p_scb[:]))
            dv.i(ve.tensor_copy(bib[:], p_bib[:]))
            for ot in range(8):
                dv.i(ve.tensor_tensor(x1[:, ot * T:(ot + 1) * T],
                                      r1[:, ot * T:(ot + 1) * T], scb[:],
                                      OP.mult))
                dv.i(ve.tensor_tensor(x1[:, ot * T:(ot + 1) * T],
                                      x1[:, ot * T:(ot + 1) * T], bib[:],
                                      OP.add))
            dv.i(ve.tensor_copy(x1b[:], x1[:]))
            x1_tick = dv.n
            sc.wait_ge(dv.h, x1_tick)
            so.i(sc.dma_start(
                hb["S_x1f"].rearrange("(dt p) l -> p dt l", p=128)[
                    :, :, t * T:(t + 1) * T],
                x1.ap().rearrange("p (dt l) -> p dt l", dt=8)), 16)
            so.i(sc.dma_start(
                hb["S_x1b"].rearrange("(dt p) l -> p dt l", p=128)[
                    :, :, t * T:(t + 1) * T],
                x1b.ap().rearrange("p (dt l) -> p dt l", dt=8)), 16)
            te.wait_ge(dv.h, x1_tick)
            for dt in range(8):
                mm = te.matmul(p_sum[0:8, :],
                               cst["gw_f"][:, dt * 8:(dt + 1) * 8],
                               x1[:, dt * T:(dt + 1) * T],
                               start=(dt == 0), stop=(dt == 7))
                if dt == 7:
                    pe.i(mm)
            ve.wait_ge(pe.h, pe.n)
            dv.i(ve.tensor_copy(lg[:], p_sum[0:8, :]))
            lg_tick = dv.n
            te.wait_ge(dv.h, lg_tick)
            for lt in range(4):
                pe.i(te.transpose(p_scb[0:128, lt * 8:(lt + 1) * 8],
                                  lg[:, lt * 128:(lt + 1) * 128],
                                  cst["idf"][0:8, 0:8]))
            ve.wait_ge(pe.h, pe.n)
            dv.i(ve.tensor_copy(lgt[:], p_scb[0:128, 0:32]))
            for lt in range(4):
                sl = lgt8[:]
                dv.i(ve.tensor_copy(lgt8[:], lgt[:, lt * 8:(lt + 1) * 8]))
                ve.wait_ge(dv.h, dv.n)
                dv.i(ve.max(top8[:], sl))
                ve.wait_ge(dv.h, dv.n)
                dv.i(ve.max_index(idxu[:], top8[:], sl))
                ve.wait_ge(dv.h, dv.n)
                dv.i(ve.tensor_copy(iv_f[:], idxu[:, 0:2]))
                ve.wait_ge(dv.h, dv.n)
                dv.i(ve.tensor_tensor(wsc[:, 0:1], top8[:, 1:2], top8[:, 0:1],
                                      OP.subtract))
                se.wait_ge(dv.h, dv.n)
                ac.i(se.activation(wsc[:, 1:2], wsc[:, 0:1], AF.Exp))
                ve.wait_ge(ac.h, ac.n)
                dv.i(ve.tensor_scalar(wsc[:, 1:2], wsc[:, 1:2], 1.0, None,
                                      OP.add))
                ve.wait_ge(dv.h, dv.n)
                dv.i(ve.reciprocal(wsc[:, 2:3], wsc[:, 1:2]))
                ve.wait_ge(dv.h, dv.n)
                dv.i(ve.tensor_scalar(wsc[:, 3:4], wsc[:, 2:3], -1.0, 1.0,
                                      OP.mult, OP.add))
                msl = m_tok[:, lt * 8:(lt + 1) * 8]
                dv.i(ve.tensor_scalar(msl, cst["iota8"][:], iv_f[:, 0:1],
                                      None, OP.is_equal))
                dv.i(ve.tensor_scalar(msl, msl, wsc[:, 2:3], None, OP.mult))
                dv.i(ve.tensor_scalar(top8[:], cst["iota8"][:], iv_f[:, 1:2],
                                      None, OP.is_equal))
                dv.i(ve.scalar_tensor_tensor(msl, top8[:], wsc[:, 3:4],
                                             msl, OP.mult, OP.add))
            mt_tick = dv.n
            if DBGY and t == 0:
                sc.wait_ge(dv.h, mt_tick)
                so.i(sc.dma_start(hb["S_yf"][0:128, 0:32], m_tok[:]), 16)
                so.i(sc.dma_start(hb["S_yf"][0:128, 32:64], lgt[:]), 16)
                so.i(sc.dma_start(hb["S_yf"][0:128, 64:72], top8[:]), 16)
                so.i(sc.dma_start(hb["S_yf"][0:128, 72:74], iv_f[:]), 16)
                so.i(sc.dma_start(hb["S_yf"][0:128, 74:78], wsc[:]), 16)
                ve.wait_ge(so.h, so.n)
            te.wait_ge(dv.h, mt_tick)
            for lt in range(4):
                pe.i(te.transpose(p_bib[0:8, lt * 128:(lt + 1) * 128],
                                  m_tok[:, lt * 8:(lt + 1) * 8],
                                  cst["idf"][:]))
            ve.wait_ge(pe.h, pe.n)
            dv.i(ve.tensor_copy(m_fm[:], p_bib[0:8, 0:512]))
            dv.i(ve.tensor_copy(mhi_b[:], m_fm[:]))
            dv.i(ve.tensor_copy(mhi_f[:], mhi_b[:]))
            dv.i(ve.tensor_tensor(mlo_f[:], m_fm[:], mhi_f[:], OP.subtract))
            m_tick = dv.n
            sc.wait_ge(dv.h, m_tick)
            so.i(sc.dma_start(hb["S_mhl"][0:8, t * T:(t + 1) * T], mhi_b[:]),
                 16)
            gp.wait_ge(dv.h, m_tick)
            sg.i(gp.dma_start(hb["S_mhl"][8:16, t * T:(t + 1) * T], mlo_f[:]),
                 16)
            s_fin.i(te.nop())
            s_fin.i(ve.nop())
            s_fin.i(se.nop())
        sy.wait_ge(so.h, so.n)
        sy.wait_ge(sg.h, sg.n)


def _phase_b(nc, ost, hb, cst):
    st = contextlib.ExitStack()
    C_ = st.enter_context
    lw = Sem(nc, ost, "b_lw")
    lx = [Sem(nc, ost, "b_lx0"), Sem(nc, ost, "b_lx1")]
    lm = Sem(nc, ost, "b_lm")
    pe = Sem(nc, ost, "b_pe")
    s_he = Sem(nc, ost, "b_he")
    s_me = Sem(nc, ost, "b_me")
    s_oe = Sem(nc, ost, "b_oe")
    s_ac = Sem(nc, ost, "b_ac")
    _clear_sems(nc, "b")
    with st, nc.Block("b"):
        sy, sc, gp, te, ve, se = (nc.sync, nc.scalar, nc.gpsimd, nc.tensor,
                                  nc.vector, nc.scalar)
        Wfc = C_(nc.sbuf_tensor("b_wfc", [128, 8 * F], BF16))
        Wpj = C_(nc.sbuf_tensor("b_wpj", [128, 32 * 1024], BF16))
        x1b = [C_(nc.sbuf_tensor(f"b_x1b{i}", [128, 8 * T], BF16))
               for i in range(2)]
        h_sb = C_(nc.sbuf_tensor("b_h", [128, 32 * T], BF16))
        eo = [C_(nc.sbuf_tensor(f"b_eo{i}", [128, 4 * T], F32))
              for i in range(2)]
        mhl = C_(nc.sbuf_tensor("b_mhl", [16, L], BF16))
        me_sb = C_(nc.sbuf_tensor("b_me", [128, T], F32))

        PS = C_(nc.psum_tensor("b_ps", [128, 8 * 512], F32))
        ph = [PS[:, 0:512], PS[:, 512:1024]]
        pa = [PS[:, 1024:1536], PS[:, 1536:2048]]
        pm = PS[:, 2048:2560]

        lm.i(sy.dma_start(mhl[:], hb["S_mhl"][:]), 16)

        def load_wfc(e):
            lw.i(gp.dma_start(
                Wfc[:].rearrange("p (dt f) -> p dt f", f=F),
                hb["fc"][e].rearrange("(dt p) f -> p dt f", p=128)), 16)
            return lw.n

        def load_wpj(e):
            lw.i(gp.dma_start(
                Wpj[:].rearrange("p (ft o) -> p ft o", o=1024),
                hb["pj"][e].rearrange("(ft p) o -> p ft o", p=128)), 16)
            return lw.n

        def load_x(n):
            e, t = divmod(n, NT)
            b = t % 2
            lx[b].i(sy.dma_start(
                x1b[b].ap().rearrange("p (dt l) -> p dt l", dt=8),
                hb["S_x1b"].rearrange("(dt p) l -> p dt l", p=128)[
                    :, :, t * T:(t + 1) * T]), 16)
            return lx[b].n

        wfc_tick = load_wfc(0)
        wpj_tick = load_wpj(0)
        x_tick = {0: load_x(0), 1: load_x(1)}
        fc_done = {}
        pj_done = {}
        acc_tick = {}

        for n in range(E * NT):
            e, t = divmod(n, NT)
            b = t % 2
            te.wait_ge(lw.h, wfc_tick)
            te.wait_ge(lx[b].h, x_tick[n])
            if n == 0:
                te.wait_ge(lm.h, 16)
            for ft in range(32):
                if n > 0 or ft >= 2:
                    te.wait_ge(s_he.h, n * 32 + ft - 1)
                for dt in range(8):
                    mm = te.matmul(ph[ft % 2],
                                   Wfc[:, dt * F + ft * 128:
                                       dt * F + (ft + 1) * 128],
                                   x1b[b][:, dt * T:(dt + 1) * T],
                                   start=(dt == 0), stop=(dt == 7))
                    if dt == 7:
                        pe.i(mm)
                se.wait_ge(pe.h, pe.n)
                if ft == 0 and n > 0:
                    se.wait_ge(pe.h, pj_done[n - 1])
                s_he.i(se.activation(h_sb[:, ft * T:(ft + 1) * T],
                                     ph[ft % 2], AF.Gelu))
            fc_done[n] = pe.n
            if n + 2 < E * NT:
                sy.wait_ge(pe.h, fc_done[n])
                x_tick[n + 2] = load_x(n + 2)
            if t == NT - 1 and e + 1 < E:
                gp.wait_ge(pe.h, fc_done[n])
                wfc_tick = load_wfc(e + 1)
            te.wait_ge(s_me.h, n)
            pe.i(te.matmul(pm, cst["msel"][:, e * 128:(e + 1) * 128],
                           mhl[:, t * T:(t + 1) * T], start=True, stop=True))
            ve.wait_ge(pe.h, pe.n)
            s_me.i(ve.tensor_copy(me_sb[:], pm))
            te.wait_ge(lw.h, wpj_tick)
            te.wait_ge(s_he.h, (n + 1) * 32)
            for ot in range(8):
                half = ot // 4
                if ot >= 2:
                    te.wait_ge(s_oe.h, n * 8 + ot - 1)
                for ft in range(32):
                    mm = te.matmul(pa[ot % 2],
                                   Wpj[:, ft * 1024 + ot * 128:
                                       ft * 1024 + (ot + 1) * 128],
                                   h_sb[:, ft * T:(ft + 1) * T],
                                   start=(ft == 0), stop=(ft == 31))
                    if ft == 31:
                        pe.i(mm)
                ve.wait_ge(pe.h, pe.n)
                ve.wait_ge(s_me.h, n + 1)
                if ot % 4 == 0 and (n - 1, half) in acc_tick:
                    ve.wait_ge(s_ac.h, acc_tick[(n - 1, half)])
                s_oe.i(ve.tensor_tensor(eo[half][:, (ot % 4) * T:
                                                  (ot % 4 + 1) * T],
                                        me_sb[:], pa[ot % 2], OP.mult))
                if ot == 3 or ot == 7:
                    gp.wait_ge(s_oe.h, s_oe.n)
                    if e > 0:
                        gp.wait_ge(s_ac.h, acc_tick[(n - NT, half)])
                    acc = hb["S_acc"].rearrange(
                        "(ot p) l -> p ot l", p=128)[
                        :, half * 4:(half + 1) * 4, t * T:(t + 1) * T]
                    src = eo[half].ap().rearrange("p (o l) -> p o l", o=4)
                    if e == 0:
                        s_ac.i(gp.dma_start(acc, src), 16)
                    else:
                        s_ac.i(gp.dma_start(acc, src, accum_op=OP.add), 16)
                    acc_tick[(n, half)] = s_ac.n
            pj_done[n] = pe.n
            if t == NT - 1 and e + 1 < E:
                gp.wait_ge(pe.h, pj_done[n])
                wpj_tick = load_wpj(e + 1)
        sy.wait_ge(s_ac.h, s_ac.n)


def _phase_c(nc, ost, hb, cst):
    st = contextlib.ExitStack()
    C_ = st.enter_context
    ld = Sem(nc, ost, "c_ld")
    pe = Sem(nc, ost, "c_pe")
    dv = Sem(nc, ost, "c_dv")
    ac = Sem(nc, ost, "c_ac")
    so = Sem(nc, ost, "c_so")
    s_te = Sem(nc, ost, "c_te")
    s_fin = Sem(nc, ost, "c_fin")
    _clear_sems(nc, "c")
    with st, nc.Block("c"):
        sy, sc, gp, te, ve, se = (nc.sync, nc.scalar, nc.gpsimd, nc.tensor,
                                  nc.vector, nc.scalar)
        x1 = [C_(nc.sbuf_tensor(f"c_x1{i}", [128, 8 * T], F32))
              for i in range(2)]
        accb = [C_(nc.sbuf_tensor(f"c_acc{i}", [128, 8 * T], F32))
                for i in range(2)]
        r2 = C_(nc.sbuf_tensor("c_r2", [128, 8 * T], F32))
        sq = C_(nc.sbuf_tensor("c_sq", [128, 8 * T], F32))
        x2 = C_(nc.sbuf_tensor("c_x2", [128, 8 * T], F32))
        rows = C_(nc.sbuf_tensor("c_rows", [1, 8 * 512], F32))
        scb = C_(nc.sbuf_tensor("c_scb", [128, 512], F32))
        bib = C_(nc.sbuf_tensor("c_bib", [128, 512], F32))
        outt = [C_(nc.sbuf_tensor(f"c_out{i}", [128, 4 * D], F32))
                for i in range(2)]

        PS = C_(nc.psum_tensor("c_ps", [128, 8 * 512], F32))
        p_sum = PS[:, 0:512]
        p_sq = PS[:, 512:1024]
        p_scb = PS[:, 1024:1536]
        p_bib = PS[:, 1536:2048]
        p_tr = [PS[:, 2048:2560], PS[:, 2560:3072]]

        def load_tile(t):
            b = t % 2
            ld.i(sy.dma_start(
                x1[b].ap().rearrange("p (dt l) -> p dt l", dt=8),
                hb["S_x1f"].rearrange("(dt p) l -> p dt l", p=128)[
                    :, :, t * T:(t + 1) * T]), 16)
            ld.i(sy.dma_start(
                accb[b].ap().rearrange("p (dt l) -> p dt l", dt=8),
                hb["S_acc"].rearrange("(dt p) l -> p dt l", p=128)[
                    :, :, t * T:(t + 1) * T]), 16)
            return ld.n

        ld_tick = [load_tile(0)]
        ntr = 0
        for t in range(NT):
            b = t % 2
            if t + 1 < NT:
                if t >= 1:
                    sy.wait_ge(s_fin.h, 3 * t)
                ld_tick.append(load_tile(t + 1))
            ve.wait_ge(ld.h, ld_tick[t])
            if t >= 1:
                ve.wait_ge(s_fin.h, 3 * t)
            for dt in range(8):
                dv.i(ve.tensor_tensor(r2[:, dt * T:(dt + 1) * T],
                                      x1[b][:, dt * T:(dt + 1) * T],
                                      accb[b][:, dt * T:(dt + 1) * T], OP.add))
            r2_tick = dv.n
            se.wait_ge(dv.h, r2_tick)
            ac.i(se.activation(sq[:], r2[:], AF.Square))
            te.wait_ge(dv.h, r2_tick)
            for dt in range(8):
                mm = te.matmul(p_sum[0:1, :], cst["ones_f"][:],
                               r2[:, dt * T:(dt + 1) * T],
                               start=(dt == 0), stop=(dt == 7))
                if dt == 7:
                    pe.i(mm)
            te.wait_ge(ac.h, ac.n)
            for dt in range(8):
                mm = te.matmul(p_sq[0:1, :], cst["ones_f"][:],
                               sq[:, dt * T:(dt + 1) * T],
                               start=(dt == 0), stop=(dt == 7))
                if dt == 7:
                    pe.i(mm)
            ve.wait_ge(pe.h, pe.n)
            dv.i(ve.tensor_scalar(rows[0:1, 0:512], p_sum[0:1, :], 1.0 / D,
                                  None, OP.mult))
            dv.i(ve.tensor_scalar(rows[0:1, 512:1024], p_sq[0:1, :], 1.0 / D,
                                  None, OP.mult))
            se.wait_ge(dv.h, dv.n)
            ac.i(se.activation(rows[0:1, 1024:1536], rows[0:1, 0:512],
                               AF.Square))
            ve.wait_ge(ac.h, ac.n)
            dv.i(ve.tensor_tensor(rows[0:1, 1536:2048], rows[0:1, 512:1024],
                                  rows[0:1, 1024:1536], OP.subtract))
            se.wait_ge(dv.h, dv.n)
            ac.i(se.activation(rows[0:1, 2048:2560], rows[0:1, 1536:2048],
                               AF.Ln, bias=cst["eps_ln"][:]))
            ac.i(se.activation(rows[0:1, 2048:2560], rows[0:1, 2048:2560],
                               AF.Exp, scale=-0.5))
            ve.wait_ge(ac.h, ac.n)
            dv.i(ve.tensor_tensor(rows[0:1, 2560:3072], rows[0:1, 0:512],
                                  rows[0:1, 2048:2560], OP.mult))
            dv.i(ve.tensor_scalar(rows[0:1, 2560:3072], rows[0:1, 2560:3072],
                                  -1.0, None, OP.mult))
            te.wait_ge(dv.h, dv.n)
            pe.i(te.matmul(p_scb[:], cst["onesr_f"][:], rows[0:1, 2048:2560],
                           start=True, stop=True))
            pe.i(te.matmul(p_bib[:], cst["onesr_f"][:], rows[0:1, 2560:3072],
                           start=True, stop=True))
            ve.wait_ge(pe.h, pe.n)
            dv.i(ve.tensor_copy(scb[:], p_scb[:]))
            dv.i(ve.tensor_copy(bib[:], p_bib[:]))
            for dt in range(8):
                dv.i(ve.tensor_tensor(x2[:, dt * T:(dt + 1) * T],
                                      r2[:, dt * T:(dt + 1) * T], scb[:],
                                      OP.mult))
                dv.i(ve.tensor_tensor(x2[:, dt * T:(dt + 1) * T],
                                      x2[:, dt * T:(dt + 1) * T], bib[:],
                                      OP.add))
            x2_tick = dv.n
            te.wait_ge(dv.h, x2_tick)
            if t >= 2:
                te.wait_ge(so.h, 16 * (t - 1))
            for lt in range(4):
                for dt in range(8):
                    if ntr >= 2:
                        te.wait_ge(s_te.h, ntr - 1)
                    p = p_tr[ntr % 2]
                    pe.i(te.transpose(p[0:128, 0:128],
                                      x2[:, dt * T + lt * 128:
                                         dt * T + (lt + 1) * 128],
                                      cst["idf"][:]))
                    ve.wait_ge(pe.h, pe.n)
                    s_te.i(ve.tensor_copy(
                        outt[b][:, lt * D + dt * 128:lt * D + (dt + 1) * 128],
                        p[0:128, 0:128]))
                    ntr += 1
            sc.wait_ge(s_te.h, ntr)
            so.i(sc.dma_start(
                hb["y"].rearrange("(tt lt p) d -> tt p lt d", p=128, lt=4)[t],
                outt[b].ap().rearrange("p (lt d) -> p lt d", lt=4)), 16)
            s_fin.i(te.nop())
            s_fin.i(ve.nop())
            s_fin.i(se.nop())
        sy.wait_ge(so.h, so.n)


def _aux_inputs():
    idn = np.eye(128, dtype=np.float32)
    maskb = np.zeros((128, 128), np.float32)
    jj, ii = np.meshgrid(np.arange(128), np.arange(128), indexing="ij")
    maskb[ii < jj] = -1e30
    iota8 = np.broadcast_to(np.arange(8, dtype=np.float32), (128, 8)).copy()
    msel = np.zeros((16, E * 128), np.float32)
    for e in range(E):
        msel[e, e * 128:(e + 1) * 128] = 1.0
        msel[8 + e, e * 128:(e + 1) * 128] = 1.0
    return {"idn_f": idn, "idn_b": idn, "maskb": maskb, "iota8": iota8,
            "msel": msel}


def make_in_maps(inputs):
    aux = _aux_inputs()
    base = {
        "w_in": np.ascontiguousarray(inputs["L0_attn_in_proj_w"]),
        "conv_w": np.ascontiguousarray(inputs["L0_attn_conv_w"]),
        "conv_b": np.ascontiguousarray(inputs["L0_attn_conv_b"]),
        "a_log": np.ascontiguousarray(inputs["L0_attn_A_log"]),
        "dparam": np.ascontiguousarray(inputs["L0_attn_Dparam"]),
        "dt_bias": np.ascontiguousarray(inputs["L0_attn_dt_bias"]),
        "norm_w": np.ascontiguousarray(inputs["L0_attn_norm_w"]),
        "w_out": np.ascontiguousarray(inputs["L0_attn_out_proj_w"]),
        "gate_w": np.ascontiguousarray(inputs["L0_ffn_gate_w"]),
        "fc_w": np.ascontiguousarray(inputs["L0_ffn_fc_w"]),
        "proj_w": np.ascontiguousarray(inputs["L0_ffn_proj_w"]),
        **aux,
    }
    return [dict(base, u=np.ascontiguousarray(inputs["L0_attn_u"][i]))
            for i in range(8)]


def kernel(**inputs):
    nc = build(dbg=False)
    in_maps = make_in_maps(inputs)
    res = run_bass_kernel_spmd(nc, in_maps, core_ids=list(range(8)))
    return np.stack([r["y"] for r in res.results], axis=0)


# version 06
# speedup vs baseline: 1.0333x; 1.0333x over previous
import contextlib
import numpy as np
import ml_dtypes
import concourse.bass as bass
import concourse.mybir as mybir
from concourse.bass_utils import run_bass_kernel_spmd

F32 = mybir.dt.float32
BF16 = mybir.dt.bfloat16
U32 = mybir.dt.uint32
AF = mybir.ActivationFunctionType
OP = mybir.AluOpType
AX = mybir.AxisListType.X

L = 4096
D = 1024
DIN = 2048
DIP = 4256
H = 32
PH = 64
NS = 64
Q = 128
NCH = 32
T = 512
NT = 8
NCT = 34
E = 8
F = 4096
EPS_RMS = 1.1920929e-07
EPS_LN = 1e-5
NW = NCT * 128
DBGY = False


_SEM_REG = []


class Sem:
    def __init__(self, nc, ctx, name):
        self.h = ctx.enter_context(nc.semaphore(name))
        self.n = 0
        _SEM_REG.append(self.h)

    def i(self, instr, amt=1):
        instr.then_inc(self.h, amt)
        self.n += amt
        return instr


def build(dbg=False, phases=("a0", "a1", "a2", "b", "c")):
    nc = bass.Bass("TRN2", target_bir_lowering=False)

    def dram(name, shape, dt=F32, kind="ExternalInput"):
        return nc.dram_tensor(name, shape, dt, kind=kind).ap()

    sk = "ExternalOutput" if dbg else "Internal"

    hb = {
        "u": dram("u", [L, D]),
        "win": dram("w_in", [DIP, D]),
        "cw": dram("conv_w", [DIP, 4]),
        "cb": dram("conv_b", [DIP]),
        "alog": dram("a_log", [H]),
        "dpar": dram("dparam", [DIN]),
        "dtb": dram("dt_bias", [H]),
        "nw": dram("norm_w", [DIN]),
        "wout": dram("w_out", [D, DIN]),
        "gw": dram("gate_w", [D, E]),
        "fc": dram("fc_w", [E, D, F]),
        "pj": dram("proj_w", [E, F, D]),
        "idf": dram("idn_f", [128, 128]),
        "idb": dram("idn_b", [128, 128]),
        "mask": dram("maskb", [128, 128]),
        "iota8": dram("iota8", [128, 8]),
        "msel": dram("msel", [16, E * 128]),
        "y": dram("y", [L, D], F32, "ExternalOutput"),
        "S_z": dram("S_z", [DIN, L], F32, sk),
        "S_xt": dram("S_xt", [L, DIN], F32, sk),
        "S_B": dram("S_B", [NS, L], BF16, sk),
        "S_Bt": dram("S_Bt", [L, NS], BF16, sk),
        "S_C": dram("S_C", [NS, L], BF16, sk),
        "S_sfm": dram("S_sfm", [H, L], F32, sk),
        "S_st": dram("S_st", [L, H], F32, sk),
        "S_wt": dram("S_wt", [L, H], F32, sk),
        "S_dtt": dram("S_dtt", [L, H], F32, sk),
        "S_wfm": dram("S_wfm", [H, L], F32, sk),
        "S_dfm": dram("S_dfm", [H, L], F32, sk),
        "S_shl": dram("S_shl", [2, NCH * H * Q], BF16, sk),
        "S_es": dram("S_es", [1, NCH * H * Q], BF16, sk),
        "S_esl": dram("S_esl", [NCH, H], F32, sk),
        "S_u": dram("S_u", [D, L], F32, sk),
        "S_g": dram("S_g", [DIN, L], BF16, sk),
        "S_x1f": dram("S_x1f", [D, L], F32, sk),
        "S_x1b": dram("S_x1b", [D, L], BF16, sk),
        "S_mhl": dram("S_mhl", [16, L], BF16, sk),
        "S_acc": dram("S_acc", [D, L], F32, sk),
        "S_winT": dram("S_winT", [128, 8 * NW], BF16, sk),
        "S_winT2": dram("S_winT2", [128, 8 * NW], BF16, sk),
        "S_yf": dram("S_yf", [L, DIN], F32, sk),
        "S_hst": dram("S_hst", [NS, H * PH], F32, sk),
    }

    ost = contextlib.ExitStack()
    C_ = ost.enter_context
    with ost:
        cst = {
            "idf": C_(nc.sbuf_tensor("c_idf", [128, 128], F32)),
            "idb": C_(nc.sbuf_tensor("c_idb", [128, 128], BF16)),
            "maskb": C_(nc.sbuf_tensor("c_maskb", [128, 128], F32)),
            "iota8": C_(nc.sbuf_tensor("c_iota8", [128, 8], F32)),
            "msel": C_(nc.sbuf_tensor("c_msel", [16, E * 128], BF16)),
            "ones_b": C_(nc.sbuf_tensor("c_onesb", [2, 128], BF16)),
            "ones_f": C_(nc.sbuf_tensor("c_onesf", [128, 1], F32)),
            "onesr_f": C_(nc.sbuf_tensor("c_onesrf", [1, 128], F32)),
            "eps_rms": C_(nc.sbuf_tensor("c_epsr", [128, 1], F32)),
            "eps_ln": C_(nc.sbuf_tensor("c_epsl", [1, 1], F32)),
            "cw_col": C_(nc.sbuf_tensor("c_cw", [128, NCT * 4], F32)),
            "cb_col": C_(nc.sbuf_tensor("c_cb", [128, NCT], F32)),
            "dtb_col": C_(nc.sbuf_tensor("c_dtbc", [H, 1], F32)),
            "cbdt_col": C_(nc.sbuf_tensor("c_cbdt", [H, 1], F32)),
            "alog_col": C_(nc.sbuf_tensor("c_alog", [H, 1], F32)),
            "nw_col": C_(nc.sbuf_tensor("c_nw", [128, 16], F32)),
            "gw_f": C_(nc.sbuf_tensor("c_gw", [128, 8 * E], F32)),
            "zeros32": C_(nc.sbuf_tensor("c_z32", [H, 128], F32)),
        }
        _prelude(nc, ost, hb, cst)
        if "a0" in phases:
            _phase_a0(nc, ost, hb, cst)
        if "a1" in phases:
            _phase_a1(nc, ost, hb, cst)
        if "a2" in phases:
            _phase_a2(nc, ost, hb, cst)
        if "b" in phases:
            _phase_b(nc, ost, hb, cst)
        if "c" in phases:
            _phase_c(nc, ost, hb, cst)

    nc.finalize()
    return nc


def _clear_sems(nc, tag):
    with nc.Block(tag + "_clr"):
        for h in _SEM_REG:
            nc.sync.sem_clear(h)
    _SEM_REG.clear()


def _prelude(nc, ost, hb, cst):
    st = contextlib.ExitStack()
    C_ = st.enter_context
    ld = Sem(nc, ost, "pre_ld")
    dn = Sem(nc, ost, "pre_dn")
    pemm = Sem(nc, ost, "pre_pe")
    _clear_sems(nc, "pre")
    with st, nc.Block("prelude"):
        sy, gp, ve, te = nc.sync, nc.gpsimd, nc.vector, nc.tensor
        ld.i(sy.dma_start(cst["idf"][:], hb["idf"][:]), 16)
        ld.i(sy.dma_start(cst["maskb"][:], hb["mask"][:]), 16)
        ld.i(sy.dma_start(cst["iota8"][:], hb["iota8"][:]), 16)
        ld.i(sy.dma_start(cst["cw_col"][:, 0:132].rearrange("p (c k) -> p c k", k=4),
                          hb["cw"][0:4224].rearrange("(c p) k -> p c k", p=128)),
             16)
        ld.i(sy.dma_start(cst["cw_col"][0:32, 132:136], hb["cw"][4224:4256]), 16)
        with nc.allow_non_contiguous_dma(reason="tiny per-partition param loads"):
            ld.i(sy.dma_start(cst["cb_col"][:, 0:33],
                              hb["cb"][0:4224].rearrange("(c p) -> p c", p=128)), 16)
        ld.i(sy.dma_start(cst["cb_col"][0:32, 33:34],
                          hb["cb"][4224:4256].rearrange("(p o) -> p o", o=1)), 16)
        ld.i(sy.dma_start(cst["dtb_col"][:],
                          hb["dtb"].rearrange("(h o) -> h o", o=1)), 16)
        ld.i(sy.dma_start(cst["alog_col"][:],
                          hb["alog"].rearrange("(h o) -> h o", o=1)), 16)
        with nc.allow_non_contiguous_dma(reason="tiny"):
            ld.i(sy.dma_start(cst["nw_col"][:],
                              hb["nw"].rearrange("(c p) -> p c", p=128)), 16)
        ld.i(sy.dma_start(cst["gw_f"][:].rearrange("p (d e) -> p d e", e=E),
                          hb["gw"].rearrange("(d p) e -> p d e", p=128)), 16)
        ld.i(gp.dma_start(cst["idb"][:], hb["idb"][:]), 16)
        ld.i(gp.dma_start(cst["msel"][:], hb["msel"][:]), 16)
        ve.memset(cst["ones_b"][:], 1.0)
        ve.memset(cst["ones_f"][:], 1.0)
        ve.memset(cst["onesr_f"][:], 1.0)
        ve.memset(cst["eps_rms"][:], EPS_RMS)
        ve.memset(cst["eps_ln"][:], EPS_LN)
        ve.memset(cst["zeros32"][:], 0.0)
        ve.wait_ge(ld.h, ld.n)
        dn.i(ve.tensor_tensor(cst["cbdt_col"][:], cst["cb_col"][0:H, 33:34],
                              cst["dtb_col"][:], OP.add))


def _phase_a0(nc, ost, hb, cst):
    st = contextlib.ExitStack()
    C_ = st.enter_context
    ld = Sem(nc, ost, "a0_ld")
    lu = Sem(nc, ost, "a0_lu")
    sg = Sem(nc, ost, "a0_sg")
    pe = Sem(nc, ost, "a0_pe")
    s_bte = Sem(nc, ost, "a0_bte")
    s_tre = Sem(nc, ost, "a0_tre")
    s_ze = Sem(nc, ost, "a0_ze")
    s_tap = Sem(nc, ost, "a0_tap")
    s_dtap = Sem(nc, ost, "a0_dtap")
    s_hal = Sem(nc, ost, "a0_hal")
    dv = Sem(nc, ost, "a0_dv")
    ac = Sem(nc, ost, "a0_ac")
    so = Sem(nc, ost, "a0_so")
    utr = Sem(nc, ost, "a0_utr")
    _clear_sems(nc, "a0")
    with st, nc.Block("a0"):
        sy, sc, gp, te, ve, se = (nc.sync, nc.scalar, nc.gpsimd, nc.tensor,
                                  nc.vector, nc.scalar)
        win_T = C_(nc.sbuf_tensor("a0_winT", [128, 8 * NW], BF16))
        u_tok = [C_(nc.sbuf_tensor("a0_utok0", [128, 4 * D], F32))]
        u_fm = C_(nc.sbuf_tensor("a0_ufm", [128, 8 * T], F32))
        u_fmb = C_(nc.sbuf_tensor("a0_ufmb", [128, 8 * T], BF16))
        zxw = [C_(nc.sbuf_tensor(f"a0_zxw{i}", [128, 515], F32)) for i in range(4)]
        conv = [C_(nc.sbuf_tensor(f"a0_conv{i}", [128, T], F32)) for i in range(4)]
        xcv = C_(nc.sbuf_tensor("a0_xcv", [128, 16 * T], F32))
        xst = [C_(nc.sbuf_tensor(f"a0_xst{i}", [128, DIN], F32)) for i in range(2)]
        halo = C_(nc.sbuf_tensor("a0_halo", [128, NCT * 3], F32))
        bc_b = C_(nc.sbuf_tensor("a0_bcb", [128, T], BF16))
        bt_st = C_(nc.sbuf_tensor("a0_btst", [128, 4 * 64], BF16))
        dt_f = C_(nc.sbuf_tensor("a0_dt", [H, T], F32))
        tmp1 = C_(nc.sbuf_tensor("a0_tmp1", [H, T], F32))
        tmp2 = C_(nc.sbuf_tensor("a0_tmp2", [H, T], F32))
        s_fm = C_(nc.sbuf_tensor("a0_sfm", [H, T], F32))
        es_fm = C_(nc.sbuf_tensor("a0_esfm", [H, T], F32))
        es_b = C_(nc.sbuf_tensor("a0_esb", [H, T], BF16))
        w_fm = C_(nc.sbuf_tensor("a0_wfm", [H, T], F32))
        shi_b = C_(nc.sbuf_tensor("a0_shib", [H, T], BF16))
        tok_st = C_(nc.sbuf_tensor("a0_tokst", [128, 12 * 32], F32))
        shi_f = C_(nc.sbuf_tensor("a0_shif", [H, T], F32))
        slo_f = C_(nc.sbuf_tensor("a0_slof", [H, T], F32))

        PSF = C_(nc.psum_tensor("a0_psf", [128, 4 * 512], F32))
        PSB = C_(nc.psum_tensor("a0_psb", [128, 2 * 1024], BF16))
        pz = [PSF[:, 0:512], PSF[:, 512:1024]]
        ptr = [PSF[:, 1024:1536], PSF[:, 1536:2048]]
        pbt = [PSB[:, 0:1024], PSB[:, 1024:2048]]

        with nc.sbuf_tensor("a0_wincm", [128, 4 * D], BF16) as win_cm:
            lu.i(sy.dma_start(
                u_tok[0].ap().rearrange("p (lt d) -> p lt d", lt=4),
                hb["u"].rearrange("(t lt p) d -> t p lt d", p=128, lt=4)[0]), 16)
            ld_u = [lu.n]
            nbth = [0]
            groups = [(0, 4), (4, 4), (8, 4), (12, 4), (16, 4), (20, 4), (24, 4), (28, 4), (32, 2)]
            wld = []
            for gi, (c0, ncg) in enumerate(groups):
                if gi > 0:
                    gp.wait_ge(pe.h, pe.n)
                nfull = min(ncg, 33 - c0) if c0 < 33 else 0
                if nfull > 0:
                    ld.i(gp.dma_start(
                        win_cm[:, 0:nfull * D].rearrange("p (c d) -> p c d", d=D),
                        hb["win"][c0 * 128:(c0 + nfull) * 128].rearrange(
                            "(c p) d -> p c d", p=128)), 16)
                if c0 + ncg > 33:
                    ld.i(gp.dma_start(win_cm[0:32, nfull * D:(nfull + 1) * D],
                                      hb["win"][4224:4256]), 16)
                wld.append(ld.n)
                te.wait_ge(ld.h, wld[-1])
                for ci in range(ncg):
                    ct = c0 + ci
                    npart = 128 if ct < 33 else 32
                    for dt in range(8):
                        nbt = nbth[0]
                        if nbt >= 2:
                            te.wait_ge(s_bte.h, nbt - 1)
                        p = pbt[nbt % 2][0:128, 0:npart]
                        pe.i(te.transpose(
                            p, win_cm[0:npart, ci * D + dt * 128:
                                      ci * D + (dt + 1) * 128],
                            cst["idb"][0:npart, 0:npart]))
                        ve.wait_ge(pe.h, pe.n)
                        s_bte.i(ve.tensor_copy(
                            win_T[:, dt * NW + ct * 128:dt * NW + ct * 128 + npart],
                            p))
                        nbth[0] += 1

        if DBGY:
            sc.wait_ge(s_bte.h, 272)
            so.i(sc.dma_start(hb["S_winT"][:], win_T[:]), 16)
        ve.memset(halo[:], 0.0)
        guards = {}
        xst_store = {}
        ntr = 0
        ze_g = 0

        for t in range(NT):
            buf = 0
            te.wait_ge(lu.h, ld_u[t])
            if t >= 1:
                ve.wait_ge(so.h, ufm_store[0])
            for lt in range(4):
                for dt in range(8):
                    if ntr >= 2:
                        te.wait_ge(s_tre.h, ntr - 1)
                    p = ptr[ntr % 2][:, 0:128]
                    pe.i(te.transpose(
                        p, u_tok[buf][:, lt * D + dt * 128:lt * D + (dt + 1) * 128],
                        cst["idf"][:]))
                    ve.wait_ge(pe.h, pe.n)
                    s_tre.i(ve.tensor_copy(
                        u_fm[:, dt * T + lt * 128:dt * T + (lt + 1) * 128], p))
                    ntr += 1
            utr.i(te.nop())
            if t + 1 < NT:
                sy.wait_ge(utr.h, t + 1)
                lu.i(sy.dma_start(
                    u_tok[0].ap().rearrange("p (lt d) -> p lt d", lt=4),
                    hb["u"].rearrange("(t lt p) d -> t p lt d",
                                      p=128, lt=4)[t + 1]), 16)
                ld_u.append(lu.n)
            ve.wait_ge(s_tre.h, ntr)
            dv.i(ve.tensor_copy(u_fmb[:], u_fm[:]))
            ufm_tick = dv.n
            sc.wait_ge(dv.h, ufm_tick)
            so.i(sc.dma_start(
                hb["S_u"].rearrange("(dt p) l -> p dt l", p=128)[
                    :, :, t * T:(t + 1) * T],
                u_fm.ap().rearrange("p (dt l) -> p dt l", dt=8)), 16)
            ufm_store = [so.n]

            te.wait_ge(dv.h, ufm_tick)
            te.wait_ge(s_bte.h, 272)
            if t > 0:
                se.wait_ge(pe.h, x_trdone)
            for ct in range(NCT):
                g = ze_g
                npart = 128 if ct < 33 else 32
                if g >= 2:
                    te.wait_ge(s_ze.h, g - 1)
                for dt in range(8):
                    mm = te.matmul(
                        pz[g % 2][0:npart, :],
                        win_T[:, dt * NW + ct * 128:dt * NW + ct * 128 + npart],
                        u_fmb[:, dt * T:(dt + 1) * T],
                        start=(dt == 0), stop=(dt == 7))
                    if dt == 7:
                        pe.i(mm)
                wbuf = zxw[g % 4]
                cbuf = xcv[:, (ct - 16) * T:(ct - 15) * T] if 16 <= ct < 32 \
                    else conv[g % 4]
                se.wait_ge(pe.h, pe.n)
                if g >= 4:
                    se.wait_ge(s_dtap.h, g - 3)
                    se.wait_ge(s_tap.h, g - 3)
                    se.wait_ge(s_hal.h, g - 3)
                if g >= 4 and (g - 4) in guards:
                    gh, gv = guards[g - 4]
                    se.wait_ge(gh, gv)
                s_ze.i(se.copy(wbuf[:, 3:515], pz[g % 2][:]))
                ve.wait_ge(s_ze.h, g + 1)
                dv.i(ve.tensor_copy(wbuf[:, 0:3], halo[:, ct * 3:ct * 3 + 3]))
                bias_ap = cst["cbdt_col"][:] if ct == 33 else \
                    cst["cb_col"][0:npart, ct:ct + 1]
                ac.i(se.activation(cbuf[0:npart, :], wbuf[0:npart, 3:515],
                                   AF.Identity,
                                   scale=cst["cw_col"][0:npart,
                                                       ct * 4 + 3:ct * 4 + 4],
                                   bias=bias_ap))
                s_hal.i(ve.tensor_copy(halo[:, ct * 3:ct * 3 + 3],
                                       wbuf[:, 512:515]))
                ve.wait_ge(ac.h, ac.n)
                s_dtap.i(ve.scalar_tensor_tensor(
                    cbuf[0:npart, :], wbuf[0:npart, 2:514],
                    cst["cw_col"][0:npart, ct * 4 + 2:ct * 4 + 3],
                    cbuf[0:npart, :], OP.mult, OP.add))
                ve.scalar_tensor_tensor(
                    cbuf[0:npart, :], wbuf[0:npart, 1:513],
                    cst["cw_col"][0:npart, ct * 4 + 1:ct * 4 + 2],
                    cbuf[0:npart, :], OP.mult, OP.add)
                s_tap.i(ve.scalar_tensor_tensor(
                    cbuf[0:npart, :], wbuf[0:npart, 0:512],
                    cst["cw_col"][0:npart, ct * 4 + 0:ct * 4 + 1],
                    cbuf[0:npart, :], OP.mult, OP.add))
                tap_n = s_tap.n

                if ct < 16:
                    sc.wait_ge(s_tap.h, tap_n)
                    so.i(sc.dma_start(hb["S_z"][ct * 128:(ct + 1) * 128,
                                                t * T:(t + 1) * T], cbuf[:]), 16)
                    guards[g] = (so.h, so.n)
                elif ct < 32:
                    if ct == 31:
                        x_tapn = s_tap.n
                elif ct == 32:
                    ve.wait_ge(s_tap.h, tap_n)
                    if t >= 1:
                        ve.wait_ge(so.h, bc_store[0])
                    dv.i(ve.tensor_copy(bc_b[:], cbuf[:]))
                    guards[g] = (dv.h, dv.n)
                    bcb_tick = dv.n
                    sc.wait_ge(dv.h, bcb_tick)
                    so.i(sc.dma_start(hb["S_B"][:, t * T:(t + 1) * T],
                                      bc_b[0:64, :]), 16)
                    so.i(sc.dma_start(hb["S_C"][:, t * T:(t + 1) * T],
                                      bc_b[64:128, :]), 16)
                    te.wait_ge(dv.h, bcb_tick)
                    for lt in range(4):
                        nbt = nbth[0]
                        if nbt >= 2:
                            te.wait_ge(s_bte.h, nbt - 1)
                        p = pbt[nbt % 2][0:128, 0:64]
                        pe.i(te.transpose(p, bc_b[0:64, lt * 128:(lt + 1) * 128],
                                          cst["idb"][0:64, 0:64]))
                        ve.wait_ge(pe.h, pe.n)
                        s_bte.i(ve.tensor_copy(bt_st[:, lt * 64:(lt + 1) * 64], p))
                        nbth[0] += 1
                    sc.wait_ge(s_bte.h, nbth[0])
                    so.i(sc.dma_start(
                        hb["S_Bt"].rearrange("(tt lt p) n -> tt p lt n",
                                             p=128, lt=4)[t],
                        bt_st.ap().rearrange("p (lt n) -> p lt n", lt=4)), 16)
                    bc_store = [so.n]
                else:
                    se.wait_ge(s_tap.h, tap_n)
                    if t >= 1:
                        se.wait_ge(so.h, dtb_store[0])
                        se.wait_ge(sg.h, dtb_sg[0])
                    ac.i(se.activation(tmp1[:], cbuf[0:H, :], AF.Abs))
                    ac.i(se.activation(tmp1[:], tmp1[:], AF.Exp, scale=-1.0))
                    ac.i(se.activation(tmp1[:], tmp1[:], AF.Ln,
                                       bias=cst["ones_f"][0:H, :]))
                    ac.i(se.activation(tmp2[:], cbuf[0:H, :], AF.Relu))
                    guards[g] = (ac.h, ac.n)
                    ve.wait_ge(ac.h, ac.n)
                    dv.i(ve.tensor_tensor(dt_f[:], tmp1[:], tmp2[:], OP.add))
                    dv.i(ve.tensor_scalar(tmp1[:], dt_f[:], cst["alog_col"][:],
                                          None, OP.mult))
                    for cc in range(4):
                        dv.i(ve.tensor_tensor_scan(
                            s_fm[:, cc * 128:(cc + 1) * 128],
                            tmp1[:, cc * 128:(cc + 1) * 128], cst["zeros32"][:],
                            0.0, OP.add, OP.add))
                    se.wait_ge(dv.h, dv.n)
                    ac.i(se.activation(es_fm[:], s_fm[:], AF.Exp))
                    for cc in range(4):
                        sl = s_fm[:, cc * 128 + 127:cc * 128 + 128]
                        dv.i(ve.tensor_scalar(tmp2[:, cc * 128:(cc + 1) * 128],
                                              s_fm[:, cc * 128:(cc + 1) * 128],
                                              -1.0, sl, OP.mult, OP.add))
                    se.wait_ge(dv.h, dv.n)
                    ac.i(se.activation(tmp2[:], tmp2[:], AF.Exp))
                    ve.wait_ge(ac.h, ac.n)
                    dv.i(ve.tensor_tensor(w_fm[:], tmp2[:], dt_f[:], OP.mult))
                    dv.i(ve.tensor_copy(shi_b[:], s_fm[:]))
                    dv.i(ve.tensor_copy(shi_f[:], shi_b[:]))
                    dv.i(ve.tensor_tensor(slo_f[:], s_fm[:], shi_f[:],
                                          OP.subtract))
                    dv.i(ve.tensor_copy(es_b[:], es_fm[:]))
                    dt_done = dv.n
                    sc.wait_ge(dv.h, dt_done)
                    for cc in range(4):
                        c = t * 4 + cc
                        with nc.allow_non_contiguous_dma(reason="tiny esl row"):
                            so.i(sc.dma_start(
                                hb["S_esl"][c:c + 1, :].rearrange("a h -> h a"),
                                es_fm[:, cc * 128 + 127:cc * 128 + 128]), 16)
                    shl_r = hb["S_shl"].rearrange("r (c h l) -> r c h l", h=H, l=Q)
                    so.i(sc.dma_start(
                        shl_r[0, t * 4:(t + 1) * 4].rearrange("c h l -> h c l"),
                        shi_b.ap().rearrange("h (c l) -> h c l", c=4)), 16)
                    gp.wait_ge(dv.h, dt_done)
                    sg.i(gp.dma_start(
                        shl_r[1, t * 4:(t + 1) * 4].rearrange("c h l -> h c l"),
                        slo_f.ap().rearrange("h (c l) -> h c l", c=4)), 16)
                    es_r = hb["S_es"].rearrange("r (c h l) -> r c h l", h=H, l=Q)
                    so.i(sc.dma_start(
                        es_r[0, t * 4:(t + 1) * 4].rearrange("c h l -> h c l"),
                        es_b.ap().rearrange("h (c l) -> h c l", c=4)), 16)
                    so.i(sc.dma_start(hb["S_sfm"][:, t * T:(t + 1) * T], s_fm[:]),
                         16)
                    so.i(sc.dma_start(hb["S_wfm"][:, t * T:(t + 1) * T], w_fm[:]),
                         16)
                    so.i(sc.dma_start(hb["S_dfm"][:, t * T:(t + 1) * T], dt_f[:]),
                         16)
                    te.wait_ge(dv.h, dt_done)
                    for cc in range(4):
                        for j, src in enumerate((s_fm, w_fm, dt_f)):
                            if ntr >= 2:
                                te.wait_ge(s_tre.h, ntr - 1)
                            p = ptr[ntr % 2][:, 0:H]
                            pe.i(te.transpose(
                                p, src[0:H, cc * 128:(cc + 1) * 128],
                                cst["idf"][0:H, 0:H]))
                            ve.wait_ge(pe.h, pe.n)
                            s_tre.i(ve.tensor_copy(
                                tok_st[:, (cc * 3 + j) * 32:(cc * 3 + j + 1) * 32],
                                p))
                            ntr += 1
                    sc.wait_ge(s_tre.h, ntr)
                    trr = tok_st.ap().rearrange("p (cc j n) -> p cc j n",
                                                cc=4, j=3)
                    for cc in range(4):
                        cgl = t * 4 + cc
                        so.i(sc.dma_start(hb["S_st"][cgl * Q:(cgl + 1) * Q, :],
                                          trr[:, cc, 0]), 16)
                        so.i(sc.dma_start(hb["S_wt"][cgl * Q:(cgl + 1) * Q, :],
                                          trr[:, cc, 1]), 16)
                        so.i(sc.dma_start(hb["S_dtt"][cgl * Q:(cgl + 1) * Q, :],
                                          trr[:, cc, 2]), 16)
                    dtb_store = [so.n]
                    dtb_sg = [sg.n]
                ze_g += 1

            te.wait_ge(s_tap.h, x_tapn)
            for lt in range(4):
                bx = lt % 2
                prev = xst_store.get((t, lt - 2)) if lt >= 2 else \
                    xst_store.get((t - 1, lt + 2))
                if prev:
                    ve.wait_ge(so.h, prev)
                for cx in range(16):
                    if ntr >= 2:
                        te.wait_ge(s_tre.h, ntr - 1)
                    p = ptr[ntr % 2][:, 0:128]
                    pe.i(te.transpose(
                        p, xcv[:, cx * T + lt * 128:cx * T + (lt + 1) * 128],
                        cst["idf"][:]))
                    ve.wait_ge(pe.h, pe.n)
                    s_tre.i(ve.tensor_copy(xst[bx][:, cx * 128:(cx + 1) * 128],
                                           p))
                    ntr += 1
                sc.wait_ge(s_tre.h, ntr)
                so.i(sc.dma_start(
                    hb["S_xt"][t * T + lt * 128:t * T + (lt + 1) * 128, :],
                    xst[bx][:]), 16)
                xst_store[(t, lt)] = so.n
            x_trdone = pe.n

        if DBGY:
            sc.wait_ge(s_tap.h, s_tap.n)
            so.i(sc.dma_start(hb["S_winT2"][:], win_T[:]), 16)
        sy.wait_ge(so.h, so.n)
        sy.wait_ge(sg.h, sg.n)


def _phase_a1(nc, ost, hb, cst):
    st = contextlib.ExitStack()
    C_ = st.enter_context
    ld2 = [Sem(nc, ost, "a1_ld0"), Sem(nc, ost, "a1_ld1")]
    pe = Sem(nc, ost, "a1_pe")
    dv = Sem(nc, ost, "a1_dv")
    ac = Sem(nc, ost, "a1_ac")
    so = Sem(nc, ost, "a1_so")
    s67 = Sem(nc, ost, "a1_s67")
    s_ye = Sem(nc, ost, "a1_ye")
    s_hup = Sem(nc, ost, "a1_hup")
    s_fin = Sem(nc, ost, "a1_fin")
    s_pl = Sem(nc, ost, "a1_pl")
    _clear_sems(nc, "a1")
    with st, nc.Block("a1"):
        sy, sc, gp, te, ve, se = (nc.sync, nc.scalar, nc.gpsimd, nc.tensor,
                                  nc.vector, nc.scalar)
        xt_f2 = [C_(nc.sbuf_tensor(f"a1_xtf2{i}", [128, DIN], F32))
                 for i in range(2)]
        z_f = [C_(nc.sbuf_tensor(f"a1_zf{i}", [128, 16 * Q], F32))
               for i in range(2)]
        B_fm = [C_(nc.sbuf_tensor(f"a1_bfm{i}", [NS, Q], BF16)) for i in range(2)]
        Cfm = [C_(nc.sbuf_tensor(f"a1_cfm{i}", [NS, Q], BF16)) for i in range(2)]
        st_c2 = [C_(nc.sbuf_tensor(f"a1_stc2{i}", [128, H], F32))
                 for i in range(2)]
        wt_c2 = [C_(nc.sbuf_tensor(f"a1_wtc2{i}", [128, H], F32))
                 for i in range(2)]
        dtt_c2 = [C_(nc.sbuf_tensor(f"a1_dttc2{i}", [128, H], F32))
                  for i in range(2)]
        shl_c = [C_(nc.sbuf_tensor(f"a1_shl{i}", [2, H * Q], BF16))
                 for i in range(2)]
        es_c = [C_(nc.sbuf_tensor(f"a1_es{i}", [1, H * Q], BF16))
                for i in range(2)]
        esl_c = [C_(nc.sbuf_tensor(f"a1_esl{i}", [1, H], F32)) for i in range(2)]
        xt_b = C_(nc.sbuf_tensor("a1_xtb", [128, DIN], BF16))
        xdt_b = C_(nc.sbuf_tensor("a1_xdtb", [128, DIN], BF16))
        bw_all = C_(nc.sbuf_tensor("a1_bwall", [128, DIN], BF16))
        btm_f = C_(nc.sbuf_tensor("a1_btmf", [128, NS], F32))
        Btm = [C_(nc.sbuf_tensor(f"a1_btm{i}", [128, NS], BF16)) for i in range(2)]
        eslb = C_(nc.sbuf_tensor("a1_eslb", [NS, H], F32))
        cbt = C_(nc.sbuf_tensor("a1_cbt", [128, Q], F32))
        Lx = [C_(nc.sbuf_tensor(f"a1_lx{i}", [128, Q], F32)) for i in range(3)]
        Lw = [C_(nc.sbuf_tensor(f"a1_lw{i}", [128, Q], F32)) for i in range(3)]
        mt = [C_(nc.sbuf_tensor(f"a1_mt{i}", [128, Q], BF16)) for i in range(3)]
        cl_all = C_(nc.sbuf_tensor("a1_clall", [NS, H * Q], BF16))
        cfm_f = C_(nc.sbuf_tensor("a1_cfmf", [NS, Q], F32))
        hst_f = C_(nc.sbuf_tensor("a1_hstf", [NS, H * PH], F32))
        hst_b = C_(nc.sbuf_tensor("a1_hstb", [NS, H * PH], BF16))
        xd = C_(nc.sbuf_tensor("a1_xd", [128, DIN], F32))
        y_full = C_(nc.sbuf_tensor("a1_yf", [128, DIN], F32))
        y2j = C_(nc.sbuf_tensor("a1_y2j", [128, DIN], F32))
        ssq = C_(nc.sbuf_tensor("a1_ssq", [128, 1], F32))
        rstd = C_(nc.sbuf_tensor("a1_rstd", [128, 1], F32))
        zs = [C_(nc.sbuf_tensor(f"a1_zs{i}", [128, DIN], F32))
              for i in range(2)]
        gated = C_(nc.sbuf_tensor("a1_gated", [128, DIN], BF16))

        PS = C_(nc.psum_tensor("a1_ps", [128, 8 * 512], F32))
        py = [PS[:, i * 512:(i + 1) * 512] for i in range(4)]
        pmix = PS[:, 4 * 512:5 * 512]
        pdh = PS[:, 5 * 512:6 * 512]
        p67 = [PS[:, 6 * 512:7 * 512], PS[:, 7 * 512:8 * 512]]

        D_tb = C_(nc.sbuf_tensor("a1_dtb2", [128, DIN], F32))
        nw_tb = C_(nc.sbuf_tensor("a1_nwtb", [128, DIN], F32))
        drow = C_(nc.sbuf_tensor("a1_drow", [1, DIN], F32))
        dhl = C_(nc.sbuf_tensor("a1_dhl", [2, DIN], BF16))
        dhf = C_(nc.sbuf_tensor("a1_dhf", [1, DIN], F32))
        dlf = C_(nc.sbuf_tensor("a1_dlf", [1, DIN], F32))
        ve.memset(hst_f[:], 0.0)
        dv.i(ve.memset(hst_b[:], 0.0))
        ld2[0].i(sy.dma_start(drow[:], hb["dpar"].rearrange("(o n) -> o n", o=1)),
                 16)
        ve.wait_ge(ld2[0].h, ld2[0].n)
        ve.tensor_copy(dhl[0:1, :], drow[:])
        ve.tensor_copy(dhf[:], dhl[0:1, :])
        dv.i(ve.tensor_tensor(dlf[:], drow[:], dhf[:], OP.subtract))
        gp.wait_ge(dv.h, dv.n)
        dv.i(gp.dma_start(dhl[1:2, :], dlf[:]), 16)
        te.wait_ge(dv.h, dv.n)
        for i in range(4):
            pe.i(te.matmul(PS[:, i * 512:(i + 1) * 512], cst["ones_b"][:],
                           dhl[:, i * 512:(i + 1) * 512], start=True, stop=True))
        ve.wait_ge(pe.h, pe.n)
        dv.i(ve.tensor_copy(D_tb[:], PS[:, 0:DIN]))
        sy.wait_ge(dv.h, dv.n)
        ld2[0].i(sy.dma_start(drow[:], hb["nw"].rearrange("(o n) -> o n", o=1)),
                 16)
        ve.wait_ge(ld2[0].h, ld2[0].n)
        ve.tensor_copy(dhl[0:1, :], drow[:])
        ve.tensor_copy(dhf[:], dhl[0:1, :])
        dv.i(ve.tensor_tensor(dlf[:], drow[:], dhf[:], OP.subtract))
        gp.wait_ge(dv.h, dv.n)
        dv.i(gp.dma_start(dhl[1:2, :], dlf[:]), 16)
        te.wait_ge(dv.h, dv.n)
        for i in range(4):
            pe.i(te.matmul(PS[:, i * 512:(i + 1) * 512], cst["ones_b"][:],
                           dhl[:, i * 512:(i + 1) * 512], start=True, stop=True))
        ve.wait_ge(pe.h, pe.n)
        dv.i(ve.tensor_copy(nw_tb[:], PS[:, 0:DIN]))

        def load_chunk(c):
            b = c % 2
            ld = ld2[b]
            ld.i(sy.dma_start(xt_f2[b][:], hb["S_xt"][c * Q:(c + 1) * Q, :]), 16)
            ld.i(sy.dma_start(
                z_f[b].ap().rearrange("p (ct l) -> p ct l", ct=16),
                hb["S_z"].rearrange("(ct p) l -> p ct l", p=128)[
                    :, :, c * Q:(c + 1) * Q]), 16)
            ld.i(sy.dma_start(B_fm[b][:], hb["S_B"][:, c * Q:(c + 1) * Q]), 16)
            ld.i(sy.dma_start(Cfm[b][:], hb["S_C"][:, c * Q:(c + 1) * Q]), 16)
            ld.i(sy.dma_start(st_c2[b][:], hb["S_st"][c * Q:(c + 1) * Q, :]), 16)
            ld.i(sy.dma_start(wt_c2[b][:], hb["S_wt"][c * Q:(c + 1) * Q, :]), 16)
            ld.i(sy.dma_start(dtt_c2[b][:], hb["S_dtt"][c * Q:(c + 1) * Q, :]),
                 16)
            ld.i(sy.dma_start(shl_c[b][:],
                              hb["S_shl"][:, c * H * Q:(c + 1) * H * Q]), 16)
            ld.i(sy.dma_start(es_c[b][:],
                              hb["S_es"][:, c * H * Q:(c + 1) * H * Q]), 16)
            ld.i(sy.dma_start(Btm[b][:], hb["S_Bt"][c * Q:(c + 1) * Q, :]), 16)
            ld.i(sy.dma_start(esl_c[b][:], hb["S_esl"][c:c + 1, :]), 16)
            return ld.n

        n67 = 0

        def rot67():
            nonlocal n67
            k = n67
            if k >= 2:
                te.wait_ge(s67.h, k - 1)
            slot = p67[k % 2]
            n67 += 1
            return slot

        ld_tick = [load_chunk(0)]
        g_store = [0]

        for c in range(NCH):
            b = c % 2
            if c + 1 < NCH:
                if c >= 1:
                    sy.wait_ge(s_fin.h, 3 * c)
                ld_tick.append(load_chunk(c + 1))

            te.wait_ge(ld2[b].h, ld_tick[c])
            if c >= 1:
                te.wait_ge(s_fin.h, 3 * c)

            ve.wait_ge(ld2[b].h, ld_tick[c])
            dv.i(ve.tensor_copy(xt_b[:], xt_f2[b][:]))
            dv.i(ve.tensor_tensor(xd[:], xt_f2[b][:], D_tb[:], OP.mult))
            dv.i(ve.tensor_tensor(
                xdt_b[:].rearrange("p (h d) -> p h d", d=PH),
                xt_f2[b][:].rearrange("p (h d) -> p h d", d=PH),
                dtt_c2[b][:].unsqueeze(2).broadcast_to([128, H, PH]),
                OP.mult))
            dv.i(ve.tensor_copy(btm_f[:], Btm[b][:]))
            dv.i(ve.tensor_tensor(
                bw_all[:].rearrange("p (h n) -> p h n", n=NS),
                btm_f[:].unsqueeze(1).broadcast_to([128, H, NS]),
                wt_c2[b][:].unsqueeze(2).broadcast_to([128, H, NS]),
                OP.mult))
            xtb_tick = dv.n
            se.wait_ge(ld2[b].h, ld_tick[c])
            if c >= 1:
                se.wait_ge(s_fin.h, 3 * c)
            ac.i(se.activation(zs[b][:], z_f[b][:], AF.Tanh, scale=0.5))
            ve.wait_ge(ac.h, ac.n)
            dv.i(ve.tensor_scalar(zs[b][:], zs[b][:], 0.5, 0.5, OP.mult,
                                  OP.add))
            dv.i(ve.tensor_tensor(zs[b][:], zs[b][:], z_f[b][:], OP.mult))
            zs_tick = dv.n

            if c >= 1:
                te.wait_ge(s_hup.h, c * 4)
            pe.i(te.matmul(pmix[0:128, 0:128], B_fm[b][:], Cfm[b][:],
                           start=True, stop=True))
            pe.i(te.matmul(pmix[0:NS, 128:128 + H], cst["onesr_f"][0:1, 0:NS],
                           esl_c[b][:], start=True, stop=True))
            ve.wait_ge(pe.h, pe.n)
            dv.i(ve.tensor_copy(cbt[:], pmix[0:128, 0:128]))
            dv.i(ve.tensor_copy(eslb[:], pmix[0:NS, 128:128 + H]))
            pmix_evac = dv.n
            dv.i(ve.tensor_copy(cfm_f[:], Cfm[b][:]))
            for q in range(8):
                slot = rot67()
                pe.i(te.matmul(slot[0:NS, :], cst["ones_b"][0:1, 0:NS],
                               es_c[b][0:1, q * 512:(q + 1) * 512],
                               start=True, stop=True))
                ve.wait_ge(pe.h, pe.n)
                s67.i(ve.tensor_tensor(
                    cl_all[:, q * 512:(q + 1) * 512]
                    .rearrange("p (h l) -> p h l", l=Q),
                    cfm_f[:].unsqueeze(1).broadcast_to([NS, 4, Q]),
                    slot[0:NS, :].rearrange("p (h l) -> p h l", l=Q), OP.mult))

            pl_base = s_pl.n
            ymm1 = {}
            ymm2 = {}
            dh_tick = {}
            exp_tick = {}
            for hh in range(H + 2):
                if hh < H:
                    h = hh
                    p_h = h % 3
                    slot = rot67()
                    pe.i(te.matmul(slot[0:128, 0:128], cst["ones_b"][:],
                                   shl_c[b][0:2, h * Q:(h + 1) * Q],
                                   start=True, stop=True))
                    bc_tick = pe.n
                    ve.wait_ge(pe.h, bc_tick)
                    if h >= 3:
                        ve.wait_ge(ac.h, exp_tick[h - 3])
                    s67.i(ve.scalar_tensor_tensor(
                        Lx[p_h][:], slot[0:128, 0:128], st_c2[b][:, h:h + 1],
                        cst["maskb"][:], OP.subtract, OP.add))
                    se.wait_ge(s67.h, s67.n)
                    if h >= 3:
                        se.wait_ge(s_pl.h, pl_base + h - 2)
                    ac.i(se.activation(Lw[p_h][:], Lx[p_h][:], AF.Exp))
                    exp_tick[h] = ac.n
                if hh >= 2:
                    g = hh - 2
                    p_g = g % 3
                    g8 = g // 8
                    ve.wait_ge(ac.h, exp_tick[g])
                    if g >= 3:
                        ve.wait_ge(pe.h, ymm1[g - 3])
                    if g == 0:
                        ve.wait_ge(dv.h, pmix_evac)
                    s_pl.i(ve.tensor_tensor(mt[p_g][:], Lw[p_g][:], cbt[:],
                                            OP.mult))
                    te.wait_ge(s_pl.h, s_pl.n)
                    if g % 8 == 0 and c >= 1:
                        te.wait_ge(s_ye.h, (c - 1) * 4 + g8 + 1)
                    pe.i(te.matmul(py[g8][:, (g % 8) * PH:(g % 8 + 1) * PH],
                                   mt[p_g][:], xdt_b[:, g * PH:(g + 1) * PH],
                                   start=(g % 8 == 0), stop=False))
                    ymm1[g] = pe.n
                    pe.i(te.matmul(py[g8][:, (g % 8) * PH:(g % 8 + 1) * PH],
                                   cl_all[:, g * Q:(g + 1) * Q],
                                   hst_b[:, g * PH:(g + 1) * PH],
                                   start=False, stop=(g % 8 == 7)))
                    ymm2[g] = pe.n
                    dhb = pdh if g8 % 2 == 0 else pmix
                    if g == 8:
                        te.wait_ge(dv.h, pmix_evac)
                    if g == 16:
                        te.wait_ge(s_hup.h, c * 4 + 1)
                    if g == 24:
                        te.wait_ge(s_hup.h, c * 4 + 2)
                    pe.i(te.matmul(dhb[0:NS, (g % 8) * PH:(g % 8 + 1) * PH],
                                   bw_all[:, g * NS:(g + 1) * NS],
                                   xt_b[:, g * PH:(g + 1) * PH],
                                   start=(g % 8 == 0), stop=True))
                    dh_tick[g] = pe.n
                    if g % 8 == 7:
                        ve.wait_ge(pe.h, ymm2[g])
                        s_ye.i(ve.tensor_tensor(
                            y_full[:, g8 * 512:(g8 + 1) * 512],
                            xd[:, g8 * 512:(g8 + 1) * 512], py[g8][:], OP.add))
                    if g in (15, 23, 31):
                        first = {15: 0, 23: 8, 31: 16}[g]
                        ve.wait_ge(pe.h, dh_tick[first + 7])
                        db = pdh if (first // 8) % 2 == 0 else pmix
                        hsl = hst_f[:, first * PH:(first + 8) * PH]
                        dv.i(ve.tensor_tensor(
                            hsl.rearrange("p (h d) -> p h d", d=PH),
                            hsl.rearrange("p (h d) -> p h d", d=PH),
                            eslb[:, first:first + 8].unsqueeze(2)
                            .broadcast_to([NS, 8, PH]), OP.mult))
                        s_hup.i(ve.tensor_tensor(hsl, hsl, db[0:NS, :], OP.add))
                    if g == 31:
                        ve.wait_ge(pe.h, dh_tick[31])
                        hsl = hst_f[:, 24 * PH:32 * PH]
                        dv.i(ve.tensor_tensor(
                            hsl.rearrange("p (h d) -> p h d", d=PH),
                            hsl.rearrange("p (h d) -> p h d", d=PH),
                            eslb[:, 24:32].unsqueeze(2)
                            .broadcast_to([NS, 8, PH]), OP.mult))
                        s_hup.i(ve.tensor_tensor(hsl, hsl, pmix[0:NS, :],
                                                 OP.add))
            ve.wait_ge(pe.h, ymm2[31])
            dv.i(ve.tensor_copy(hst_b[:], hst_f[:]))

            if DBGY:
                sc.wait_ge(s_ye.h, s_ye.n)
                sc.wait_ge(dv.h, dv.n)
                so.i(sc.dma_start(hb["S_yf"][c * Q:(c + 1) * Q, :], y_full[:]), 16)
                if c == 0:
                    so.i(sc.dma_start(hb["S_hst"][:], hst_f[:]), 16)
                ve.wait_ge(so.h, so.n)

            se.wait_ge(dv.h, dv.n)
            ac.i(se.activation(y2j[:], y_full[:], AF.Square, accum_out=ssq[:]))
            ac.i(se.activation(rstd[:], ssq[:], AF.Ln, scale=1.0 / DIN,
                               bias=cst["eps_rms"][:]))
            ac.i(se.activation(rstd[:], rstd[:], AF.Exp, scale=-0.5))
            ve.wait_ge(ac.h, ac.n)
            dv.i(ve.tensor_scalar(y_full[:], y_full[:], rstd[:], None, OP.mult))
            dv.i(ve.tensor_tensor(y_full[:], y_full[:], nw_tb[:], OP.mult))
            ysc_tick = dv.n

            te.wait_ge(dv.h, ysc_tick)
            if c >= 1:
                ve.wait_ge(so.h, g_store[0])
            for grp in range(4):
                slot = rot67()
                for q in range(4):
                    ctl = grp * 4 + q
                    pe.i(te.transpose(slot[0:128, q * 128:(q + 1) * 128],
                                      y_full[:, ctl * 128:(ctl + 1) * 128],
                                      cst["idf"][:]))
                ve.wait_ge(pe.h, pe.n)
                ve.wait_ge(dv.h, zs_tick)
                s67.i(ve.tensor_tensor(
                    gated[:, grp * 512:(grp + 1) * 512],
                    zs[b][:, grp * 512:(grp + 1) * 512],
                    slot[0:128, 0:512], OP.mult))
            sc.wait_ge(s67.h, n67)
            so.i(sc.dma_start(
                hb["S_g"].rearrange("(ct p) l -> p ct l", p=128)[
                    :, :, c * Q:(c + 1) * Q],
                gated.ap().rearrange("p (ct l) -> p ct l", ct=16)), 16)
            g_store = [so.n]

            s_fin.i(te.nop())
            s_fin.i(ve.nop())
            s_fin.i(se.nop())

        sy.wait_ge(so.h, so.n)


def _phase_a2(nc, ost, hb, cst):
    st = contextlib.ExitStack()
    C_ = st.enter_context
    ld = Sem(nc, ost, "a2_ld")
    lw = Sem(nc, ost, "a2_lw")
    pe = Sem(nc, ost, "a2_pe")
    dv = Sem(nc, ost, "a2_dv")
    ac = Sem(nc, ost, "a2_ac")
    so = Sem(nc, ost, "a2_so")
    sg = Sem(nc, ost, "a2_sg")
    s_wte = Sem(nc, ost, "a2_wte")
    s_oe = Sem(nc, ost, "a2_oe")
    s_fin = Sem(nc, ost, "a2_fin")
    _clear_sems(nc, "a2")
    with st, nc.Block("a2"):
        sy, sc, gp, te, ve, se = (nc.sync, nc.scalar, nc.gpsimd, nc.tensor,
                                  nc.vector, nc.scalar)
        WoT = C_(nc.sbuf_tensor("a2_wot", [128, 16 * 1024], BF16))
        g_sb = [C_(nc.sbuf_tensor(f"a2_g{i}", [128, 16 * T], BF16))
                for i in range(2)]
        u_fm = [C_(nc.sbuf_tensor(f"a2_u{i}", [128, 8 * T], F32))
                for i in range(2)]
        r1 = C_(nc.sbuf_tensor("a2_r1", [128, 8 * T], F32))
        sq = C_(nc.sbuf_tensor("a2_sq", [128, 8 * T], F32))
        x1 = C_(nc.sbuf_tensor("a2_x1", [128, 8 * T], F32))
        x1b = C_(nc.sbuf_tensor("a2_x1b", [128, 8 * T], BF16))
        rows = C_(nc.sbuf_tensor("a2_rows", [1, 8 * 512], F32))
        scb = C_(nc.sbuf_tensor("a2_scb", [128, 512], F32))
        bib = C_(nc.sbuf_tensor("a2_bib", [128, 512], F32))
        lg = C_(nc.sbuf_tensor("a2_lg", [8, 512], F32))
        lgt = C_(nc.sbuf_tensor("a2_lgt", [128, 4 * 8], F32))
        lgt8 = C_(nc.sbuf_tensor("a2_lgt8", [128, 8], F32))
        top8 = C_(nc.sbuf_tensor("a2_top8", [128, 8], F32))
        idxu = C_(nc.sbuf_tensor("a2_idx", [128, 8], U32))
        iv_f = C_(nc.sbuf_tensor("a2_ivf", [128, 2], F32))
        wsc = C_(nc.sbuf_tensor("a2_wsc", [128, 4], F32))
        m_tok = C_(nc.sbuf_tensor("a2_mtok", [128, 32], F32))
        m_fm = C_(nc.sbuf_tensor("a2_mfm", [8, 512], F32))
        mhi_b = C_(nc.sbuf_tensor("a2_mhib", [8, 512], BF16))
        mhi_f = C_(nc.sbuf_tensor("a2_mhif", [8, 512], F32))
        mlo_f = C_(nc.sbuf_tensor("a2_mlof", [8, 512], F32))

        PS = C_(nc.psum_tensor("a2_ps", [128, 6 * 512], F32))
        PSB2 = C_(nc.psum_tensor("a2_psb", [128, 2 * 1024], BF16))
        po = [PS[:, 0:512], PS[:, 512:1024]]
        p_sum = PS[:, 1024:1536]
        p_sq = PS[:, 1536:2048]
        p_scb = PS[:, 2048:2560]
        p_bib = PS[:, 2560:3072]
        pbt2 = [PSB2[:, 0:1024], PSB2[:, 1024:2048]]

        with nc.sbuf_tensor("a2_wcm", [128, 4 * DIN], BF16) as wcm:
            nwt = 0
            for gi in range(2):
                if gi > 0:
                    gp.wait_ge(pe.h, pe.n)
                lw.i(gp.dma_start(
                    wcm[:].rearrange("p (c d) -> p c d", d=DIN),
                    hb["wout"][gi * 512:(gi + 1) * 512].rearrange(
                        "(c p) d -> p c d", p=128)), 16)
                te.wait_ge(lw.h, lw.n)
                for ci in range(4):
                    ot = gi * 4 + ci
                    for dint in range(16):
                        if nwt >= 2:
                            te.wait_ge(s_wte.h, nwt - 1)
                        p = pbt2[nwt % 2][0:128, 0:128]
                        pe.i(te.transpose(
                            p, wcm[:, ci * DIN + dint * 128:
                                   ci * DIN + (dint + 1) * 128],
                            cst["idb"][:]))
                        ve.wait_ge(pe.h, pe.n)
                        s_wte.i(ve.tensor_copy(
                            WoT[:, dint * 1024 + ot * 128:
                                dint * 1024 + (ot + 1) * 128], p))
                        nwt += 1

        def load_tile(t):
            b = t % 2
            ld.i(sy.dma_start(
                g_sb[b].ap().rearrange("p (ct l) -> p ct l", ct=16),
                hb["S_g"].rearrange("(ct p) l -> p ct l", p=128)[
                    :, :, t * T:(t + 1) * T]), 16)
            ld.i(sy.dma_start(
                u_fm[b].ap().rearrange("p (dt l) -> p dt l", dt=8),
                hb["S_u"].rearrange("(dt p) l -> p dt l", p=128)[
                    :, :, t * T:(t + 1) * T]), 16)
            return ld.n

        ld_tick = [load_tile(0)]
        for t in range(NT):
            b = t % 2
            if t + 1 < NT:
                if t >= 1:
                    sy.wait_ge(s_fin.h, 3 * t)
                ld_tick.append(load_tile(t + 1))
            te.wait_ge(ld.h, ld_tick[t])
            if t >= 1:
                te.wait_ge(s_fin.h, 3 * t)
            for ot in range(8):
                if ot >= 2:
                    te.wait_ge(s_oe.h, (t * 8) + ot - 1)
                for dint in range(16):
                    mm = te.matmul(po[ot % 2],
                                   WoT[:, dint * 1024 + ot * 128:
                                       dint * 1024 + (ot + 1) * 128],
                                   g_sb[b][:, dint * T:(dint + 1) * T],
                                   start=(dint == 0), stop=(dint == 15))
                    if dint == 15:
                        pe.i(mm)
                ve.wait_ge(pe.h, pe.n)
                s_oe.i(ve.tensor_tensor(r1[:, ot * T:(ot + 1) * T],
                                        u_fm[b][:, ot * T:(ot + 1) * T],
                                        po[ot % 2], OP.add))
            se.wait_ge(s_oe.h, s_oe.n)
            ac.i(se.activation(sq[:], r1[:], AF.Square))
            te.wait_ge(s_oe.h, s_oe.n)
            for ot in range(8):
                mm = te.matmul(p_sum[0:1, :], cst["ones_f"][:],
                               r1[:, ot * T:(ot + 1) * T],
                               start=(ot == 0), stop=(ot == 7))
                if ot == 7:
                    pe.i(mm)
            te.wait_ge(ac.h, ac.n)
            for ot in range(8):
                mm = te.matmul(p_sq[0:1, :], cst["ones_f"][:],
                               sq[:, ot * T:(ot + 1) * T],
                               start=(ot == 0), stop=(ot == 7))
                if ot == 7:
                    pe.i(mm)
            ve.wait_ge(pe.h, pe.n)
            dv.i(ve.tensor_scalar(rows[0:1, 0:512], p_sum[0:1, :], 1.0 / D,
                                  None, OP.mult))
            dv.i(ve.tensor_scalar(rows[0:1, 512:1024], p_sq[0:1, :], 1.0 / D,
                                  None, OP.mult))
            se.wait_ge(dv.h, dv.n)
            ac.i(se.activation(rows[0:1, 1024:1536], rows[0:1, 0:512],
                               AF.Square))
            ve.wait_ge(ac.h, ac.n)
            dv.i(ve.tensor_tensor(rows[0:1, 1536:2048], rows[0:1, 512:1024],
                                  rows[0:1, 1024:1536], OP.subtract))
            se.wait_ge(dv.h, dv.n)
            ac.i(se.activation(rows[0:1, 2048:2560], rows[0:1, 1536:2048],
                               AF.Ln, bias=cst["eps_ln"][:]))
            ac.i(se.activation(rows[0:1, 2048:2560], rows[0:1, 2048:2560],
                               AF.Exp, scale=-0.5))
            ve.wait_ge(ac.h, ac.n)
            dv.i(ve.tensor_tensor(rows[0:1, 2560:3072], rows[0:1, 0:512],
                                  rows[0:1, 2048:2560], OP.mult))
            dv.i(ve.tensor_scalar(rows[0:1, 2560:3072], rows[0:1, 2560:3072],
                                  -1.0, None, OP.mult))
            te.wait_ge(dv.h, dv.n)
            pe.i(te.matmul(p_scb[:], cst["onesr_f"][:],
                           rows[0:1, 2048:2560], start=True, stop=True))
            pe.i(te.matmul(p_bib[:], cst["onesr_f"][:],
                           rows[0:1, 2560:3072], start=True, stop=True))
            ve.wait_ge(pe.h, pe.n)
            dv.i(ve.tensor_copy(scb[:], p_scb[:]))
            dv.i(ve.tensor_copy(bib[:], p_bib[:]))
            for ot in range(8):
                dv.i(ve.tensor_tensor(x1[:, ot * T:(ot + 1) * T],
                                      r1[:, ot * T:(ot + 1) * T], scb[:],
                                      OP.mult))
                dv.i(ve.tensor_tensor(x1[:, ot * T:(ot + 1) * T],
                                      x1[:, ot * T:(ot + 1) * T], bib[:],
                                      OP.add))
            dv.i(ve.tensor_copy(x1b[:], x1[:]))
            x1_tick = dv.n
            sc.wait_ge(dv.h, x1_tick)
            so.i(sc.dma_start(
                hb["S_x1f"].rearrange("(dt p) l -> p dt l", p=128)[
                    :, :, t * T:(t + 1) * T],
                x1.ap().rearrange("p (dt l) -> p dt l", dt=8)), 16)
            so.i(sc.dma_start(
                hb["S_x1b"].rearrange("(dt p) l -> p dt l", p=128)[
                    :, :, t * T:(t + 1) * T],
                x1b.ap().rearrange("p (dt l) -> p dt l", dt=8)), 16)
            te.wait_ge(dv.h, x1_tick)
            for dt in range(8):
                mm = te.matmul(p_sum[0:8, :],
                               cst["gw_f"][:, dt * 8:(dt + 1) * 8],
                               x1[:, dt * T:(dt + 1) * T],
                               start=(dt == 0), stop=(dt == 7))
                if dt == 7:
                    pe.i(mm)
            ve.wait_ge(pe.h, pe.n)
            dv.i(ve.tensor_copy(lg[:], p_sum[0:8, :]))
            lg_tick = dv.n
            te.wait_ge(dv.h, lg_tick)
            for lt in range(4):
                pe.i(te.transpose(p_scb[0:128, lt * 8:(lt + 1) * 8],
                                  lg[:, lt * 128:(lt + 1) * 128],
                                  cst["idf"][0:8, 0:8]))
            ve.wait_ge(pe.h, pe.n)
            dv.i(ve.tensor_copy(lgt[:], p_scb[0:128, 0:32]))
            for lt in range(4):
                sl = lgt8[:]
                dv.i(ve.tensor_copy(lgt8[:], lgt[:, lt * 8:(lt + 1) * 8]))
                ve.wait_ge(dv.h, dv.n)
                dv.i(ve.max(top8[:], sl))
                ve.wait_ge(dv.h, dv.n)
                dv.i(ve.max_index(idxu[:], top8[:], sl))
                ve.wait_ge(dv.h, dv.n)
                dv.i(ve.tensor_copy(iv_f[:], idxu[:, 0:2]))
                ve.wait_ge(dv.h, dv.n)
                dv.i(ve.tensor_tensor(wsc[:, 0:1], top8[:, 1:2], top8[:, 0:1],
                                      OP.subtract))
                se.wait_ge(dv.h, dv.n)
                ac.i(se.activation(wsc[:, 1:2], wsc[:, 0:1], AF.Exp))
                ve.wait_ge(ac.h, ac.n)
                dv.i(ve.tensor_scalar(wsc[:, 1:2], wsc[:, 1:2], 1.0, None,
                                      OP.add))
                ve.wait_ge(dv.h, dv.n)
                dv.i(ve.reciprocal(wsc[:, 2:3], wsc[:, 1:2]))
                ve.wait_ge(dv.h, dv.n)
                dv.i(ve.tensor_scalar(wsc[:, 3:4], wsc[:, 2:3], -1.0, 1.0,
                                      OP.mult, OP.add))
                msl = m_tok[:, lt * 8:(lt + 1) * 8]
                dv.i(ve.tensor_scalar(msl, cst["iota8"][:], iv_f[:, 0:1],
                                      None, OP.is_equal))
                dv.i(ve.tensor_scalar(msl, msl, wsc[:, 2:3], None, OP.mult))
                dv.i(ve.tensor_scalar(top8[:], cst["iota8"][:], iv_f[:, 1:2],
                                      None, OP.is_equal))
                dv.i(ve.scalar_tensor_tensor(msl, top8[:], wsc[:, 3:4],
                                             msl, OP.mult, OP.add))
            mt_tick = dv.n
            if DBGY and t == 0:
                sc.wait_ge(dv.h, mt_tick)
                so.i(sc.dma_start(hb["S_yf"][0:128, 0:32], m_tok[:]), 16)
                so.i(sc.dma_start(hb["S_yf"][0:128, 32:64], lgt[:]), 16)
                so.i(sc.dma_start(hb["S_yf"][0:128, 64:72], top8[:]), 16)
                so.i(sc.dma_start(hb["S_yf"][0:128, 72:74], iv_f[:]), 16)
                so.i(sc.dma_start(hb["S_yf"][0:128, 74:78], wsc[:]), 16)
                ve.wait_ge(so.h, so.n)
            te.wait_ge(dv.h, mt_tick)
            for lt in range(4):
                pe.i(te.transpose(p_bib[0:8, lt * 128:(lt + 1) * 128],
                                  m_tok[:, lt * 8:(lt + 1) * 8],
                                  cst["idf"][:]))
            ve.wait_ge(pe.h, pe.n)
            dv.i(ve.tensor_copy(m_fm[:], p_bib[0:8, 0:512]))
            dv.i(ve.tensor_copy(mhi_b[:], m_fm[:]))
            dv.i(ve.tensor_copy(mhi_f[:], mhi_b[:]))
            dv.i(ve.tensor_tensor(mlo_f[:], m_fm[:], mhi_f[:], OP.subtract))
            m_tick = dv.n
            sc.wait_ge(dv.h, m_tick)
            so.i(sc.dma_start(hb["S_mhl"][0:8, t * T:(t + 1) * T], mhi_b[:]),
                 16)
            gp.wait_ge(dv.h, m_tick)
            sg.i(gp.dma_start(hb["S_mhl"][8:16, t * T:(t + 1) * T], mlo_f[:]),
                 16)
            s_fin.i(te.nop())
            s_fin.i(ve.nop())
            s_fin.i(se.nop())
        sy.wait_ge(so.h, so.n)
        sy.wait_ge(sg.h, sg.n)


def _phase_b(nc, ost, hb, cst):
    st = contextlib.ExitStack()
    C_ = st.enter_context
    lw = Sem(nc, ost, "b_lw")
    lx = [Sem(nc, ost, "b_lx0"), Sem(nc, ost, "b_lx1")]
    lm = Sem(nc, ost, "b_lm")
    pe = Sem(nc, ost, "b_pe")
    s_he = Sem(nc, ost, "b_he")
    s_me = Sem(nc, ost, "b_me")
    s_oe = Sem(nc, ost, "b_oe")
    s_ac = Sem(nc, ost, "b_ac")
    _clear_sems(nc, "b")
    with st, nc.Block("b"):
        sy, sc, gp, te, ve, se = (nc.sync, nc.scalar, nc.gpsimd, nc.tensor,
                                  nc.vector, nc.scalar)
        Wfc = C_(nc.sbuf_tensor("b_wfc", [128, 8 * F], BF16))
        Wpj = C_(nc.sbuf_tensor("b_wpj", [128, 32 * 1024], BF16))
        x1b = [C_(nc.sbuf_tensor(f"b_x1b{i}", [128, 8 * T], BF16))
               for i in range(2)]
        h_sb = C_(nc.sbuf_tensor("b_h", [128, 32 * T], BF16))
        eo = [C_(nc.sbuf_tensor(f"b_eo{i}", [128, 4 * T], F32))
              for i in range(2)]
        mhl = C_(nc.sbuf_tensor("b_mhl", [16, L], BF16))
        me_sb = C_(nc.sbuf_tensor("b_me", [128, T], F32))

        PS = C_(nc.psum_tensor("b_ps", [128, 8 * 512], F32))
        ph = [PS[:, 0:512], PS[:, 512:1024]]
        pa = [PS[:, 1024:1536], PS[:, 1536:2048]]
        pm = PS[:, 2048:2560]

        lm.i(sy.dma_start(mhl[:], hb["S_mhl"][:]), 16)

        def load_wfc(e):
            lw.i(gp.dma_start(
                Wfc[:].rearrange("p (dt f) -> p dt f", f=F),
                hb["fc"][e].rearrange("(dt p) f -> p dt f", p=128)), 16)
            return lw.n

        def load_wpj(e):
            lw.i(gp.dma_start(
                Wpj[:].rearrange("p (ft o) -> p ft o", o=1024),
                hb["pj"][e].rearrange("(ft p) o -> p ft o", p=128)), 16)
            return lw.n

        def load_x(n):
            e, t = divmod(n, NT)
            b = t % 2
            lx[b].i(sy.dma_start(
                x1b[b].ap().rearrange("p (dt l) -> p dt l", dt=8),
                hb["S_x1b"].rearrange("(dt p) l -> p dt l", p=128)[
                    :, :, t * T:(t + 1) * T]), 16)
            return lx[b].n

        wfc_tick = load_wfc(0)
        wpj_tick = load_wpj(0)
        x_tick = {0: load_x(0), 1: load_x(1)}
        fc_done = {}
        pj_done = {}
        acc_tick = {}

        for n in range(E * NT):
            e, t = divmod(n, NT)
            b = t % 2
            te.wait_ge(lw.h, wfc_tick)
            te.wait_ge(lx[b].h, x_tick[n])
            if n == 0:
                te.wait_ge(lm.h, 16)
            for ft in range(32):
                if n > 0 or ft >= 2:
                    te.wait_ge(s_he.h, n * 32 + ft - 1)
                for dt in range(8):
                    mm = te.matmul(ph[ft % 2],
                                   Wfc[:, dt * F + ft * 128:
                                       dt * F + (ft + 1) * 128],
                                   x1b[b][:, dt * T:(dt + 1) * T],
                                   start=(dt == 0), stop=(dt == 7))
                    if dt == 7:
                        pe.i(mm)
                se.wait_ge(pe.h, pe.n)
                if ft == 0 and n > 0:
                    se.wait_ge(pe.h, pj_done[n - 1])
                s_he.i(se.activation(h_sb[:, ft * T:(ft + 1) * T],
                                     ph[ft % 2], AF.Gelu))
            fc_done[n] = pe.n
            if n + 2 < E * NT:
                sy.wait_ge(pe.h, fc_done[n])
                x_tick[n + 2] = load_x(n + 2)
            if t == NT - 1 and e + 1 < E:
                gp.wait_ge(pe.h, fc_done[n])
                wfc_tick = load_wfc(e + 1)
            te.wait_ge(s_me.h, n)
            pe.i(te.matmul(pm, cst["msel"][:, e * 128:(e + 1) * 128],
                           mhl[:, t * T:(t + 1) * T], start=True, stop=True))
            ve.wait_ge(pe.h, pe.n)
            s_me.i(ve.tensor_copy(me_sb[:], pm))
            te.wait_ge(lw.h, wpj_tick)
            te.wait_ge(s_he.h, (n + 1) * 32)
            for ot in range(8):
                half = ot // 4
                if ot >= 2:
                    te.wait_ge(s_oe.h, n * 8 + ot - 1)
                for ft in range(32):
                    mm = te.matmul(pa[ot % 2],
                                   Wpj[:, ft * 1024 + ot * 128:
                                       ft * 1024 + (ot + 1) * 128],
                                   h_sb[:, ft * T:(ft + 1) * T],
                                   start=(ft == 0), stop=(ft == 31))
                    if ft == 31:
                        pe.i(mm)
                ve.wait_ge(pe.h, pe.n)
                ve.wait_ge(s_me.h, n + 1)
                if ot % 4 == 0 and (n - 1, half) in acc_tick:
                    ve.wait_ge(s_ac.h, acc_tick[(n - 1, half)])
                s_oe.i(ve.tensor_tensor(eo[half][:, (ot % 4) * T:
                                                  (ot % 4 + 1) * T],
                                        me_sb[:], pa[ot % 2], OP.mult))
                if ot == 3 or ot == 7:
                    gp.wait_ge(s_oe.h, s_oe.n)
                    if e > 0:
                        gp.wait_ge(s_ac.h, acc_tick[(n - NT, half)])
                    acc = hb["S_acc"].rearrange(
                        "(ot p) l -> p ot l", p=128)[
                        :, half * 4:(half + 1) * 4, t * T:(t + 1) * T]
                    src = eo[half].ap().rearrange("p (o l) -> p o l", o=4)
                    if e == 0:
                        s_ac.i(gp.dma_start(acc, src), 16)
                    else:
                        s_ac.i(gp.dma_start(acc, src, accum_op=OP.add), 16)
                    acc_tick[(n, half)] = s_ac.n
            pj_done[n] = pe.n
            if t == NT - 1 and e + 1 < E:
                gp.wait_ge(pe.h, pj_done[n])
                wpj_tick = load_wpj(e + 1)
        sy.wait_ge(s_ac.h, s_ac.n)


def _phase_c(nc, ost, hb, cst):
    st = contextlib.ExitStack()
    C_ = st.enter_context
    ld = Sem(nc, ost, "c_ld")
    pe = Sem(nc, ost, "c_pe")
    dv = Sem(nc, ost, "c_dv")
    ac = Sem(nc, ost, "c_ac")
    so = Sem(nc, ost, "c_so")
    s_te = Sem(nc, ost, "c_te")
    s_fin = Sem(nc, ost, "c_fin")
    _clear_sems(nc, "c")
    with st, nc.Block("c"):
        sy, sc, gp, te, ve, se = (nc.sync, nc.scalar, nc.gpsimd, nc.tensor,
                                  nc.vector, nc.scalar)
        x1 = [C_(nc.sbuf_tensor(f"c_x1{i}", [128, 8 * T], F32))
              for i in range(2)]
        accb = [C_(nc.sbuf_tensor(f"c_acc{i}", [128, 8 * T], F32))
                for i in range(2)]
        r2 = C_(nc.sbuf_tensor("c_r2", [128, 8 * T], F32))
        sq = C_(nc.sbuf_tensor("c_sq", [128, 8 * T], F32))
        x2 = C_(nc.sbuf_tensor("c_x2", [128, 8 * T], F32))
        rows = C_(nc.sbuf_tensor("c_rows", [1, 8 * 512], F32))
        scb = C_(nc.sbuf_tensor("c_scb", [128, 512], F32))
        bib = C_(nc.sbuf_tensor("c_bib", [128, 512], F32))
        outt = [C_(nc.sbuf_tensor(f"c_out{i}", [128, 4 * D], F32))
                for i in range(2)]

        PS = C_(nc.psum_tensor("c_ps", [128, 8 * 512], F32))
        p_sum = PS[:, 0:512]
        p_sq = PS[:, 512:1024]
        p_scb = PS[:, 1024:1536]
        p_bib = PS[:, 1536:2048]
        p_tr = [PS[:, 2048:2560], PS[:, 2560:3072]]

        def load_tile(t):
            b = t % 2
            ld.i(sy.dma_start(
                x1[b].ap().rearrange("p (dt l) -> p dt l", dt=8),
                hb["S_x1f"].rearrange("(dt p) l -> p dt l", p=128)[
                    :, :, t * T:(t + 1) * T]), 16)
            ld.i(sy.dma_start(
                accb[b].ap().rearrange("p (dt l) -> p dt l", dt=8),
                hb["S_acc"].rearrange("(dt p) l -> p dt l", p=128)[
                    :, :, t * T:(t + 1) * T]), 16)
            return ld.n

        ld_tick = [load_tile(0)]
        ntr = 0
        for t in range(NT):
            b = t % 2
            if t + 1 < NT:
                if t >= 1:
                    sy.wait_ge(s_fin.h, 3 * t)
                ld_tick.append(load_tile(t + 1))
            ve.wait_ge(ld.h, ld_tick[t])
            if t >= 1:
                ve.wait_ge(s_fin.h, 3 * t)
            for dt in range(8):
                dv.i(ve.tensor_tensor(r2[:, dt * T:(dt + 1) * T],
                                      x1[b][:, dt * T:(dt + 1) * T],
                                      accb[b][:, dt * T:(dt + 1) * T], OP.add))
            r2_tick = dv.n
            se.wait_ge(dv.h, r2_tick)
            ac.i(se.activation(sq[:], r2[:], AF.Square))
            te.wait_ge(dv.h, r2_tick)
            for dt in range(8):
                mm = te.matmul(p_sum[0:1, :], cst["ones_f"][:],
                               r2[:, dt * T:(dt + 1) * T],
                               start=(dt == 0), stop=(dt == 7))
                if dt == 7:
                    pe.i(mm)
            te.wait_ge(ac.h, ac.n)
            for dt in range(8):
                mm = te.matmul(p_sq[0:1, :], cst["ones_f"][:],
                               sq[:, dt * T:(dt + 1) * T],
                               start=(dt == 0), stop=(dt == 7))
                if dt == 7:
                    pe.i(mm)
            ve.wait_ge(pe.h, pe.n)
            dv.i(ve.tensor_scalar(rows[0:1, 0:512], p_sum[0:1, :], 1.0 / D,
                                  None, OP.mult))
            dv.i(ve.tensor_scalar(rows[0:1, 512:1024], p_sq[0:1, :], 1.0 / D,
                                  None, OP.mult))
            se.wait_ge(dv.h, dv.n)
            ac.i(se.activation(rows[0:1, 1024:1536], rows[0:1, 0:512],
                               AF.Square))
            ve.wait_ge(ac.h, ac.n)
            dv.i(ve.tensor_tensor(rows[0:1, 1536:2048], rows[0:1, 512:1024],
                                  rows[0:1, 1024:1536], OP.subtract))
            se.wait_ge(dv.h, dv.n)
            ac.i(se.activation(rows[0:1, 2048:2560], rows[0:1, 1536:2048],
                               AF.Ln, bias=cst["eps_ln"][:]))
            ac.i(se.activation(rows[0:1, 2048:2560], rows[0:1, 2048:2560],
                               AF.Exp, scale=-0.5))
            ve.wait_ge(ac.h, ac.n)
            dv.i(ve.tensor_tensor(rows[0:1, 2560:3072], rows[0:1, 0:512],
                                  rows[0:1, 2048:2560], OP.mult))
            dv.i(ve.tensor_scalar(rows[0:1, 2560:3072], rows[0:1, 2560:3072],
                                  -1.0, None, OP.mult))
            te.wait_ge(dv.h, dv.n)
            pe.i(te.matmul(p_scb[:], cst["onesr_f"][:], rows[0:1, 2048:2560],
                           start=True, stop=True))
            pe.i(te.matmul(p_bib[:], cst["onesr_f"][:], rows[0:1, 2560:3072],
                           start=True, stop=True))
            ve.wait_ge(pe.h, pe.n)
            dv.i(ve.tensor_copy(scb[:], p_scb[:]))
            dv.i(ve.tensor_copy(bib[:], p_bib[:]))
            for dt in range(8):
                dv.i(ve.tensor_tensor(x2[:, dt * T:(dt + 1) * T],
                                      r2[:, dt * T:(dt + 1) * T], scb[:],
                                      OP.mult))
                dv.i(ve.tensor_tensor(x2[:, dt * T:(dt + 1) * T],
                                      x2[:, dt * T:(dt + 1) * T], bib[:],
                                      OP.add))
            x2_tick = dv.n
            te.wait_ge(dv.h, x2_tick)
            if t >= 2:
                te.wait_ge(so.h, 16 * (t - 1))
            for lt in range(4):
                for dt in range(8):
                    if ntr >= 2:
                        te.wait_ge(s_te.h, ntr - 1)
                    p = p_tr[ntr % 2]
                    pe.i(te.transpose(p[0:128, 0:128],
                                      x2[:, dt * T + lt * 128:
                                         dt * T + (lt + 1) * 128],
                                      cst["idf"][:]))
                    ve.wait_ge(pe.h, pe.n)
                    s_te.i(ve.tensor_copy(
                        outt[b][:, lt * D + dt * 128:lt * D + (dt + 1) * 128],
                        p[0:128, 0:128]))
                    ntr += 1
            sc.wait_ge(s_te.h, ntr)
            so.i(sc.dma_start(
                hb["y"].rearrange("(tt lt p) d -> tt p lt d", p=128, lt=4)[t],
                outt[b].ap().rearrange("p (lt d) -> p lt d", lt=4)), 16)
            s_fin.i(te.nop())
            s_fin.i(ve.nop())
            s_fin.i(se.nop())
        sy.wait_ge(so.h, so.n)


def _aux_inputs():
    idn = np.eye(128, dtype=np.float32)
    maskb = np.zeros((128, 128), np.float32)
    jj, ii = np.meshgrid(np.arange(128), np.arange(128), indexing="ij")
    maskb[ii < jj] = -1e30
    iota8 = np.broadcast_to(np.arange(8, dtype=np.float32), (128, 8)).copy()
    msel = np.zeros((16, E * 128), np.float32)
    for e in range(E):
        msel[e, e * 128:(e + 1) * 128] = 1.0
        msel[8 + e, e * 128:(e + 1) * 128] = 1.0
    return {"idn_f": idn, "idn_b": idn, "maskb": maskb, "iota8": iota8,
            "msel": msel}


def make_in_maps(inputs):
    aux = _aux_inputs()
    base = {
        "w_in": np.ascontiguousarray(inputs["L0_attn_in_proj_w"]),
        "conv_w": np.ascontiguousarray(inputs["L0_attn_conv_w"]),
        "conv_b": np.ascontiguousarray(inputs["L0_attn_conv_b"]),
        "a_log": np.ascontiguousarray(inputs["L0_attn_A_log"]),
        "dparam": np.ascontiguousarray(inputs["L0_attn_Dparam"]),
        "dt_bias": np.ascontiguousarray(inputs["L0_attn_dt_bias"]),
        "norm_w": np.ascontiguousarray(inputs["L0_attn_norm_w"]),
        "w_out": np.ascontiguousarray(inputs["L0_attn_out_proj_w"]),
        "gate_w": np.ascontiguousarray(inputs["L0_ffn_gate_w"]),
        "fc_w": np.ascontiguousarray(inputs["L0_ffn_fc_w"]),
        "proj_w": np.ascontiguousarray(inputs["L0_ffn_proj_w"]),
        **aux,
    }
    return [dict(base, u=np.ascontiguousarray(inputs["L0_attn_u"][i]))
            for i in range(8)]


def kernel(**inputs):
    nc = build(dbg=False)
    in_maps = make_in_maps(inputs)
    res = run_bass_kernel_spmd(nc, in_maps, core_ids=list(range(8)))
    return np.stack([r["y"] for r in res.results], axis=0)
